# version 01
# baseline (speedup 1.0000x reference)
import numpy as np
from contextlib import ExitStack
import concourse.bass as bass
import concourse.mybir as mybir
from concourse.bass_utils import run_bass_kernel_spmd

F32 = mybir.dt.float32
BF16 = mybir.dt.bfloat16
FP8 = mybir.dt.float8e4
U8 = mybir.dt.uint8
AF = mybir.ActivationFunctionType
ALU = mybir.AluOpType
AX = mybir.AxisListType

NCORES = 8
B_CORE = 2
S = 2048
D = 1024
T_ROWS = B_CORE * S
NTILES = T_ROWS // 128


ENGINE_ORDER = ("sync", "vector", "scalar", "tensor", "gpsimd")

class Plan:
    def __init__(self):
        self.steps = []
        self.semcnt = {}

    def cur(self, sem):
        return self.semcnt.get(sem, 0)

    def do(self, eng, fn, waits=None, inc=None, inck=1):
        ws = []
        for w in (waits or []):
            if isinstance(w, tuple):
                if w[1] > 0:
                    ws.append((w[0], w[1]))
            else:
                v = self.cur(w)
                if v > 0:
                    ws.append((w, v))
        incs = []
        if inc is not None:
            self.semcnt[inc] = self.cur(inc) + inck
            incs.append((inc, inck))
        self.steps.append((eng, fn, ws, incs))
        return self.cur(inc) if inc is not None else None

    def dma(self, eng, out, in_, waits=None, inc=None):
        return self.do(eng, mk_dma(out, in_), waits=waits, inc=inc, inck=16)

    def emit(self, nc, ctx):
        sems = {name: ctx.enter_context(nc.semaphore(name)) for name in self.semcnt}
        block = ctx.enter_context(nc.Block())
        for engname in ENGINE_ORDER:
            mysteps = [s for s in self.steps if s[0] == engname]
            if not mysteps:
                continue
            def body(eng, mysteps=mysteps):
                for _, fn, ws, incs in mysteps:
                    for (s, v) in ws:
                        eng.wait_ge(sems[s], v)
                    instr = fn(eng)
                    for (s, k) in incs:
                        instr.then_inc(sems[s], k)
            getattr(block, engname)(body)


def mk_dma(out, in_):
    return lambda e: e.dma_start(out=out, in_=in_)



def emit_layernorm(p, sb, tile_f32, xln_out, tmp, cols, sem, eng="vector"):
    m, v, r = cols
    n = tile_f32.shape[1] if hasattr(tile_f32, 'shape') else 1024
    p.do(eng, lambda e, a=tile_f32, o=m: e.reduce_sum(o[:], a[:], axis=AX.X), waits=[sem], inc=sem)
    p.do(eng, lambda e, o=m: e.tensor_scalar_mul(o[:], o[:], 1.0 / 1024), waits=[sem], inc=sem)
    p.do(eng, lambda e, a=tile_f32, o=tmp: e.tensor_scalar_sub(o[:], a[:], m[:]), waits=[sem], inc=sem)
    p.do("scalar", lambda e: e.activation(xln_out[:], tmp[:], AF.Square, accum_out=v[:]), waits=[sem], inc=sem)
    p.do(eng, lambda e: e.tensor_scalar(v[:], v[:], 1.0 / 1024, 1e-5, ALU.mult, ALU.add), waits=[sem], inc=sem)
    p.do("scalar", lambda e: e.activation(v[:], v[:], AF.Sqrt), waits=[sem], inc=sem)
    p.do(eng, lambda e: e.reciprocal(r[:], v[:]), waits=[sem], inc=sem)
    p.do(eng, lambda e: e.tensor_scalar_mul(xln_out[:], tmp[:], r[:]), waits=[sem], inc=sem)


def emit_quant_fp6(p, q, names, sem):
    src, dst = q['src'], q['dst']
    amax, sc, rsc = q['amax'], q['sc'], q['rsc']
    y, ay, mhi, q1, f8, w32, d1 = q['y'], q['ay'], q['mhi'], q['q1'], q['f8'], q['w32'], q['d1']
    n = 1024
    nb = n // 32
    def v3(t, steps):
        return bass.AP(t[:].tensor, 0, steps)
    src3 = v3(src, [[n, 128], [32, nb], [1, 32]])
    y3 = v3(y, [[n, 128], [32, nb], [1, 32]])
    sc3 = v3(sc, [[nb, 128], [1, nb], [0, 32]])
    rsc3 = v3(rsc, [[nb, 128], [1, nb], [0, 32]])
    q13 = v3(q1, [[n, 128], [32, nb], [1, 32]])
    dst3 = v3(dst, [[n, 128], [32, nb], [1, 32]])

    p.do("vector", lambda e: e.tensor_reduce(amax[:], src3, AX.X, ALU.max, apply_absolute_value=True), waits=[sem], inc=sem)
    p.do("vector", lambda e: e.tensor_scalar_mul(sc[:], amax[:], float(np.float32(1.0) / np.float32(7.5))), waits=[sem], inc=sem)
    p.do("vector", lambda e: e.tensor_single_scalar(sc[:], sc[:], 1e-30, ALU.max), waits=[sem], inc=sem)
    p.do("vector", lambda e: e.reciprocal(rsc[:], sc[:]), waits=[sem], inc=sem)
    p.do("vector", lambda e: e.tensor_tensor(y3, src3, rsc3, ALU.mult), waits=[sem], inc=sem)
    p.do("scalar", lambda e: e.activation(ay[:], y[:], AF.Abs), waits=[sem], inc=sem)
    p.do("vector", lambda e: e.tensor_single_scalar(mhi[:], ay[:], 1.9375, ALU.is_ge), waits=[sem], inc=sem)
    p.do("vector", lambda e: e.tensor_scalar_add(q1[:], y[:], 1572864.0), waits=[sem], inc=sem)
    p.do("vector", lambda e: e.tensor_scalar_add(q1[:], q1[:], -1572864.0), waits=[sem], inc=sem)
    p.do("vector", lambda e: e.tensor_scalar_mul(f8[:], y[:], 32.0), waits=[sem], inc=sem)
    p.do("vector", lambda e: e.tensor_scalar_mul(w32[:], f8[:], 1.0 / 32), waits=[sem], inc=sem)
    p.do("vector", lambda e: e.tensor_sub(d1[:], w32[:], q1[:]), waits=[sem], inc=sem)
    p.do("vector", lambda e: e.tensor_mul(d1[:], d1[:], mhi[:]), waits=[sem], inc=sem)
    p.do("vector", lambda e: e.tensor_add(q1[:], q1[:], d1[:]), waits=[sem], inc=sem)
    p.do("vector", lambda e: e.tensor_tensor(dst3, q13, sc3, ALU.mult), waits=[sem], inc=sem)



def build_blockD():
    nc = bass.Bass("TRN2", target_bir_lowering=False)
    x_d = nc.dram_tensor("x", [T_ROWS, D], F32, kind="ExternalInput").ap()
    w_d = nc.dram_tensor("w", [D, D], F32, kind="ExternalInput").ap()
    b_d = nc.dram_tensor("b", [1, D], F32, kind="ExternalInput").ap()
    id_d = nc.dram_tensor("idmat", [128, 128], BF16, kind="ExternalInput").ap()
    y_d = nc.dram_tensor("y", [T_ROWS, D], F32, kind="ExternalOutput").ap()

    p = Plan()
    ctx = ExitStack()
    with ctx:
        sb = lambda name, shp, dt=F32: ctx.enter_context(nc.sbuf_tensor(name, shp, dt))
        ps = lambda name, shp, dt=F32: ctx.enter_context(nc.psum_tensor(name, shp, dt))

        ident = sb("ident", [128, 128], BF16)
        ones_bf = sb("ones_bf", [1, 128], BF16)
        bias_bf = sb("bias_bf", [1, D], BF16)
        wdT = sb("wdT", [128, 8 * D], BF16)
        wtile = sb("wtile", [128, D])
        wd = sb("wd", [128, D], BF16)
        q = dict(src=wtile, dst=wd,
                 amax=sb("amax", [128, 32]), sc=sb("sc", [128, 32]), rsc=sb("rsc", [128, 32]),
                 y=sb("qy", [128, D]), ay=sb("qay", [128, D]), mhi=sb("qmhi", [128, D]),
                 q1=sb("qq1", [128, D]), f8=sb("qf8", [128, D], FP8), w32=sb("qw32", [128, D]),
                 d1=sb("qd1", [128, D]))
        ps_t = ps("ps_t", [128, 128], BF16)
        xtile = sb("xtile", [128, D])
        xln = sb("xln", [128, D])
        tmp = sb("tmp", [128, D])
        xd = sb("xd", [128, D], BF16)
        xdT = sb("xdT", [128, D], BF16)
        cols = (sb("c_m", [128, 1]), sb("c_v", [128, 1]), sb("c_r", [128, 1]))
        ps_o = [ps("ps_o0", [128, 512]), ps("ps_o1", [128, 512])]
        out_sb = sb("out_sb", [128, D])

        p.dma("sync", ident[:], id_d, inc="ld")
        p.dma("gpsimd", bias_bf[:], b_d, inc="ldg")
        p.do("vector", lambda e: e.memset(ones_bf[:], 1.0), inc="c")

        for wi in range(8):
            p.dma("sync", wtile[:], w_d[wi * 128:(wi + 1) * 128, :], waits=["wfree"], inc="ld")
            p.do("vector", lambda e: e.engine_nop(), waits=["ld"], inc="c")
            emit_quant_fp6(p, q, f"w{wi}", "c")
            for wj in range(8):
                p.do("tensor", (lambda wj=wj: (lambda e: nc.tensor.transpose(
                    ps_t[:], wd[:, wj * 128:(wj + 1) * 128], ident[:])))(),
                    waits=["c", "ld", "evt"], inc="tt")
                p.do("vector", (lambda wi=wi, wj=wj: (lambda e: e.tensor_copy(
                    wdT[:, wj * 1024 + wi * 128: wj * 1024 + (wi + 1) * 128], ps_t[:])))(),
                    waits=["tt"], inc="evt")
            p.do("vector", lambda e: e.engine_nop(), waits=["c", "evt"], inc="wfree", inck=16)

        for tt in range(NTILES):
            xs = x_d[tt * 128:(tt + 1) * 128, :]
            p.dma("sync", xtile[:], xs, waits=[("xfree", p.cur("xfree"))], inc="ld")
            p.do("vector", lambda e: e.engine_nop(), waits=["ld"], inc="c")
            emit_layernorm(p, sb, xtile, xln, tmp, cols, "c")
            q2 = dict(q); q2['src'] = xln; q2['dst'] = xd
            emit_quant_fp6(p, q2, f"x{tt}", "c")
            for wj in range(8):
                p.do("tensor", (lambda wj=wj: (lambda e: nc.tensor.transpose(
                    ps_t[:], xd[:, wj * 128:(wj + 1) * 128], ident[:])))(),
                    waits=["c", "evt"], inc="tt")
                p.do("vector", (lambda wj=wj: (lambda e: e.tensor_copy(
                    xdT[:, wj * 128:(wj + 1) * 128], ps_t[:])))(),
                    waits=["tt"], inc="evt")
            for mh in range(2):
                for kc in range(8):
                    p.do("tensor", (lambda mh=mh, kc=kc: (lambda e: nc.tensor.matmul(
                        ps_o[mh][:], xdT[:, kc * 128:(kc + 1) * 128],
                        wdT[:, kc * 1024 + mh * 512: kc * 1024 + mh * 512 + 512],
                        start=(kc == 0), stop=False)))(),
                        waits=(["evt", "mmdone"] if kc == 0 else []), inc=None)
                p.do("tensor", (lambda mh=mh: (lambda e: nc.tensor.matmul(
                    ps_o[mh][:], ones_bf[:], bias_bf[:, mh * 512:mh * 512 + 512],
                    start=False, stop=True)))(),
                    waits=(["ldg"] if tt == 0 and mh == 0 else []), inc="mm")
                p.do("vector", (lambda mh=mh: (lambda e: e.tensor_add(
                    out_sb[:, mh * 512:mh * 512 + 512], ps_o[mh][:],
                    xtile[:, mh * 512:mh * 512 + 512])))(),
                    waits=["mm", ("st", p.cur("st"))], inc="mmdone")
            p.do("vector", lambda e: e.engine_nop(), waits=["mmdone"], inc="xfree", inck=16)
            p.dma("gpsimd", y_d[tt * 128:(tt + 1) * 128, :], out_sb[:],
                  waits=[("mmdone", p.cur("mmdone"))], inc="st")
        p.do("gpsimd", lambda e: e.engine_nop(), waits=["st"], inc=None)

        p.emit(nc, ctx)
    return nc





TC = 256
NTC = S // TC

def build_blockC():
    nc = bass.Bass("TRN2", target_bir_lowering=False)
    x_d = nc.dram_tensor("x", [T_ROWS, D], F32, kind="ExternalInput").ap()
    ipT_d = nc.dram_tensor("ipT_dr", [1024, 4096], F32, kind="ExternalInput").ap()
    opT_d = nc.dram_tensor("opT_dr", [2048, 1024], F32, kind="ExternalInput").ap()
    xpT_d = nc.dram_tensor("xpT_dr", [2048, 96], F32, kind="ExternalInput").ap()
    dpT_d = nc.dram_tensor("dpT_dr", [64, 2048], F32, kind="ExternalInput").ap()
    convw_d = nc.dram_tensor("convw_dr", [128, 64], F32, kind="ExternalInput").ap()
    convb_d = nc.dram_tensor("convb_dr", [128, 16], F32, kind="ExternalInput").ap()
    dtb_d = nc.dram_tensor("dtb_dr", [128, 16], F32, kind="ExternalInput").ap()
    alog_d = nc.dram_tensor("alog_dr", [128, 256], F32, kind="ExternalInput").ap()
    dvec_d = nc.dram_tensor("dvec_dr", [128, 16], F32, kind="ExternalInput").ap()
    nw_d = nc.dram_tensor("nw_dr", [128, 8], F32, kind="ExternalInput").ap()
    id_d = nc.dram_tensor("idmat", [128, 128], BF16, kind="ExternalInput").ap()
    sel_d = nc.dram_tensor("selmat", [32, 4096], BF16, kind="ExternalInput").ap()
    y_d = nc.dram_tensor("y", [T_ROWS, D], F32, kind="ExternalOutput").ap()

    p = Plan()
    ctx = ExitStack()
    with ctx:
        sb = lambda name, shp, dt=F32: ctx.enter_context(nc.sbuf_tensor(name, shp, dt))
        ps = lambda name, shp, dt=F32: ctx.enter_context(nc.psum_tensor(name, shp, dt))

        ident = sb("ident", [128, 128], BF16)
        selm = sb("selm", [32, 4096], BF16)
        convw = sb("convw", [128, 64]); convb = sb("convb", [128, 16])
        dtb = sb("dtb", [128, 16]); alog = sb("alog", [128, 256])
        dvec = sb("dvec", [128, 16]); nw = sb("nw", [128, 8])
        aneg = sb("aneg", [128, 256])
        opT = sb("opT", [128, 16 * 1024], BF16)
        xpT = sb("xpT", [128, 16 * 96], BF16)
        dpT = sb("dpT", [64, 2048], BF16)
        ipT = sb("ipT", [128, 8 * 2048], BF16)
        hfm = sb("hfm", [128, 8 * TC], BF16)
        y3bf = sb("y3bf", [128, 16 * TC], BF16)
        xpart = sb("xpart", [128, 16 * (TC + 3)])
        xc = sb("xc", [128, 16 * TC])
        zsl = sb("zsl", [128, 16 * TC])
        dts = sb("dts", [128, 16 * TC])
        ysl = sb("ysl", [128, 16 * TC])
        xdbl = sb("xdbl", [96, TC])
        bcs = sb("bcs", [32, TC], BF16)
        states = sb("states", [128, 256])
        x_tm = [sb(f"x_tm{i}", [128, D]) for i in range(TC // 128)]
        xln = sb("xln", [128, D]); tmp = sb("tmp", [128, D])
        h_bf = sb("h_bf", [128, D], BF16)
        cols = (sb("c_m", [128, 1]), sb("c_v", [128, 1]), sb("c_r", [128, 1]))
        s2c = sb("s2c", [128, 1]); rmc = sb("rmc", [128, 1])
        xc_bf = sb("xc_bf", [128, TC], BF16)
        dtpre = sb("dtpre", [64, TC], BF16)
        pre_t = sb("pre_t", [128, TC]); at_t = sb("at_t", [128, TC])
        et_t = sb("et_t", [128, TC])
        Bb = sb("Bb", [128, TC]); Cb = sb("Cb", [128, TC])
        dA_t = sb("dA_t", [128, TC]); dBu_t = sb("dBu_t", [128, TC])
        h_t = sb("h_t", [128, TC]); tp_t = sb("tp_t", [128, TC])

        ps_t = ps("ps_t", [128, 128], BF16)
        ps_a = ps("ps_a", [128, TC])
        ps_96 = ps("ps_96", [96, TC])
        ps_bc = ps("ps_bc", [128, TC])
        ps_o = [ps("ps_o0", [128, 512]), ps("ps_o1", [128, 512])]

        TCP = TC + 3

        for dst, src in [(ident, id_d), (selm, sel_d)]:
            p.dma("sync", dst[:], src, inc="ld")
        for dst, src in [(convw, convw_d), (convb, convb_d), (dtb, dtb_d),
                         (alog, alog_d), (dvec, dvec_d), (nw, nw_d)]:
            p.dma("sync", dst[:], src, inc="ld")
        for kc in range(16):
            p.dma("gpsimd", opT[:, kc * 1024:(kc + 1) * 1024],
                  opT_d[kc * 128:(kc + 1) * 128, :], inc="ld")
            p.dma("gpsimd", xpT[:, kc * 96:(kc + 1) * 96],
                  xpT_d[kc * 128:(kc + 1) * 128, :], inc="ld")
        p.dma("gpsimd", dpT[:], dpT_d, inc="ld")
        p.do("scalar", lambda e: e.activation(aneg[:], alog[:], AF.Exp), waits=["ld"], inc="c")
        p.do("vector", lambda e: e.tensor_scalar_mul(aneg[:], aneg[:], -1.0), waits=["c", "ld"], inc="c")

        for b in range(B_CORE):
            p.do("vector", lambda e: e.memset(states[:], 0.0), waits=["c"], inc="c")
            p.do("vector", lambda e: e.memset(xpart[:], 0.0), waits=["c"], inc="c")
            for tc in range(NTC):
                row0 = b * S + tc * TC
                if tc > 0:
                    for dti in range(16):
                        p.do("vector", (lambda dti=dti: (lambda e: e.tensor_copy(
                            xpart[:, dti * TCP:dti * TCP + 3],
                            xpart[:, dti * TCP + TC:dti * TCP + TC + 3])))(),
                            waits=["c"], inc="c")
                for ts in range(TC // 128):
                    p.dma("sync", x_tm[ts][:], x_d[row0 + ts * 128: row0 + (ts + 1) * 128, :],
                          waits=[("c", p.cur("c")), ("mmoF", p.cur("mmoF")), ("st", p.cur("st"))], inc="ld")
                    p.do("vector", lambda e: e.engine_nop(), waits=["ld"], inc="c")
                    emit_layernorm(p, sb, x_tm[ts], xln, tmp, cols, "c")
                    p.do("scalar", lambda e: e.activation(tmp[:], xln[:], AF.Square, accum_out=s2c[:]), waits=["c"], inc="c")
                    p.do("scalar", lambda e: e.activation(s2c[:], s2c[:], AF.Sqrt), waits=["c"], inc="c")
                    p.do("vector", lambda e: e.tensor_scalar(s2c[:], s2c[:], 1.0 / 32, 1e-5, ALU.mult, ALU.add), waits=["c"], inc="c")
                    p.do("vector", lambda e: e.reciprocal(rmc[:], s2c[:]), waits=["c"], inc="c")
                    p.do("vector", lambda e: e.tensor_scalar_mul(h_bf[:], xln[:], rmc[:]), waits=["c"], inc="c")
                    for kc in range(8):
                        p.do("tensor", (lambda kc=kc: (lambda e: nc.tensor.transpose(
                            ps_t[:], h_bf[:, kc * 128:(kc + 1) * 128], ident[:])))(),
                            waits=["c", "evt"], inc="tt")
                        p.do("vector", (lambda kc=kc, ts=ts: (lambda e: e.tensor_scalar_mul(
                            hfm[:, kc * TC + ts * 128: kc * TC + ts * 128 + 128],
                            ps_t[:], nw[:, kc:kc + 1])))(),
                            waits=["tt"], inc="evt")
                for half in range(2):
                    for dc in range(8):
                        p.dma("gpsimd", ipT[:, dc * 2048:(dc + 1) * 2048],
                              ipT_d[dc * 128:(dc + 1) * 128, half * 2048:(half + 1) * 2048],
                              waits=[("mmF", p.cur("mmF"))], inc="ldw")
                    for oc in range(16):
                        jc = half * 16 + oc
                        for dc in range(8):
                            p.do("tensor", (lambda dc=dc, oc=oc: (lambda e: nc.tensor.matmul(
                                ps_a[:], ipT[:, dc * 2048 + oc * 128: dc * 2048 + oc * 128 + 128],
                                hfm[:, dc * TC:(dc + 1) * TC],
                                start=(dc == 0), stop=(dc == 7))))(),
                                waits=(["ldw", "evt", "mmF"] if dc == 0 else []),
                                inc=("mm" if dc == 7 else None))
                        if jc < 16:
                            p.do("vector", (lambda jc=jc: (lambda e: e.tensor_copy(
                                xpart[:, jc * TCP + 3: jc * TCP + 3 + TC], ps_a[:])))(),
                                waits=["mm"], inc="mmF")
                        else:
                            p.do("scalar", (lambda jc=jc: (lambda e: e.activation(
                                zsl[:, (jc - 16) * TC:(jc - 16) * TC + TC], ps_a[:], AF.Silu)))(),
                                waits=["mm"], inc="mmF")
                for dti in range(16):
                    p.do("vector", (lambda dti=dti: (lambda e: e.tensor_scalar(
                        dA_t[:], xpart[:, dti * TCP: dti * TCP + TC],
                        convw[:, dti * 4: dti * 4 + 1], convb[:, dti:dti + 1],
                        ALU.mult, ALU.add)))(), waits=["c", "mmF"], inc="c")
                    for k in range(1, 4):
                        p.do("vector", (lambda dti=dti, k=k: (lambda e: e.scalar_tensor_tensor(
                            dA_t[:], xpart[:, dti * TCP + k: dti * TCP + k + TC],
                            convw[:, dti * 4 + k: dti * 4 + k + 1], dA_t[:],
                            ALU.mult, ALU.add)))(), waits=["c"], inc="c")
                    p.do("scalar", (lambda dti=dti: (lambda e: e.activation(
                        xc[:, dti * TC:(dti + 1) * TC], dA_t[:], AF.Silu)))(),
                        waits=["c"], inc="c")
                for dti in range(16):
                    p.do("vector", (lambda dti=dti: (lambda e: e.tensor_copy(
                        xc_bf[:], xc[:, dti * TC:(dti + 1) * TC])))(),
                        waits=["c", ("mm96", p.cur("mm96"))], inc="c")
                    p.do("tensor", (lambda dti=dti: (lambda e: nc.tensor.matmul(
                        ps_96[:], xpT[:, dti * 96:(dti + 1) * 96], xc_bf[:],
                        start=(dti == 0), stop=(dti == 15))))(),
                        waits=["c"], inc="mm96")
                p.do("vector", lambda e: e.tensor_copy(xdbl[:], ps_96[:]), waits=[("mm96", p.cur("mm96"))], inc="c")
                p.do("vector", lambda e: e.tensor_copy(dtpre[:], xdbl[0:64, :]), waits=["c"], inc="c")
                p.do("vector", lambda e: e.tensor_copy(bcs[:], xdbl[64:96, :]), waits=["c"], inc="c")
                for dc in range(16):
                    p.do("tensor", (lambda dc=dc: (lambda e: nc.tensor.matmul(
                        ps_a[:], dpT[:, dc * 128:(dc + 1) * 128], dtpre[:],
                        start=True, stop=True)))(), waits=["c", "dtF"], inc="mmdt")
                    p.do("vector", (lambda dc=dc: (lambda e: e.tensor_scalar_add(
                        pre_t[:], ps_a[:], dtb[:, dc:dc + 1])))(), waits=["mmdt"], inc="dtF")
                    p.do("scalar", lambda e: e.activation(at_t[:], pre_t[:], AF.Abs), waits=["dtF"], inc="c")
                    p.do("scalar", lambda e: e.activation(et_t[:], at_t[:], AF.Exp, scale=-1.0), waits=["c"], inc="c")
                    p.do("scalar", lambda e: e.activation(at_t[:], et_t[:], AF.Ln, bias=1.0), waits=["c"], inc="c")
                    p.do("vector", lambda e: e.tensor_relu(et_t[:], pre_t[:]), waits=["c"], inc="c")
                    p.do("vector", (lambda dc=dc: (lambda e: e.tensor_add(
                        dts[:, dc * TC:(dc + 1) * TC], at_t[:], et_t[:])))(),
                        waits=["c"], inc="c")
                for dti in range(16):
                    p.do("vector", (lambda dti=dti: (lambda e: e.tensor_mul(
                        xpart[:, dti * TCP:dti * TCP + TC], dts[:, dti * TC:(dti + 1) * TC],
                        xc[:, dti * TC:(dti + 1) * TC])))(), waits=["c"], inc="c")
                for n in range(16):
                    p.do("tensor", (lambda n=n: (lambda e: nc.tensor.matmul(
                        ps_bc[:], selm[:, n * 128:(n + 1) * 128], bcs[:],
                        start=True, stop=True)))(), waits=["c", "bcF"], inc="mmbc")
                    p.do("vector", lambda e: e.tensor_copy(Bb[:], ps_bc[:]), waits=["mmbc"], inc="bcF")
                    p.do("tensor", (lambda n=n: (lambda e: nc.tensor.matmul(
                        ps_bc[:], selm[:, (16 + n) * 128:(17 + n) * 128], bcs[:],
                        start=True, stop=True)))(), waits=["bcF"], inc="mmbc")
                    p.do("vector", lambda e: e.tensor_copy(Cb[:], ps_bc[:]), waits=["mmbc"], inc="bcF")
                    for dti in range(16):
                        idx = dti * 16 + n
                        p.do("scalar", (lambda dti=dti, n=n: (lambda e: e.activation(
                            dA_t[:], dts[:, dti * TC:(dti + 1) * TC], AF.Exp,
                            scale=aneg[:, dti * 16 + n: dti * 16 + n + 1])))(),
                            waits=["c"], inc="c")
                        p.do("vector", (lambda dti=dti: (lambda e: e.tensor_mul(
                            dBu_t[:], xpart[:, dti * TCP:dti * TCP + TC], Bb[:])))(),
                            waits=["c", "bcF"], inc="c")
                        p.do("vector", (lambda idx=idx: (lambda e: e.tensor_tensor_scan(
                            h_t[:], dA_t[:], dBu_t[:], states[:, idx:idx + 1],
                            ALU.mult, ALU.add)))(), waits=["c"], inc="c")
                        p.do("vector", (lambda idx=idx: (lambda e: e.tensor_copy(
                            states[:, idx:idx + 1], h_t[:, TC - 1:TC])))(),
                            waits=["c"], inc="c")
                        if n == 0:
                            p.do("vector", (lambda dti=dti: (lambda e: e.tensor_mul(
                                ysl[:, dti * TC:(dti + 1) * TC], h_t[:], Cb[:])))(),
                                waits=["c"], inc="c")
                        else:
                            p.do("vector", lambda e: e.tensor_mul(tp_t[:], h_t[:], Cb[:]), waits=["c"], inc="c")
                            p.do("vector", (lambda dti=dti: (lambda e: e.tensor_add(
                                ysl[:, dti * TC:(dti + 1) * TC],
                                ysl[:, dti * TC:(dti + 1) * TC], tp_t[:])))(),
                                waits=["c"], inc="c")
                for dti in range(16):
                    p.do("vector", (lambda dti=dti: (lambda e: e.scalar_tensor_tensor(
                        tp_t[:], xc[:, dti * TC:(dti + 1) * TC], dvec[:, dti:dti + 1],
                        ysl[:, dti * TC:(dti + 1) * TC], ALU.mult, ALU.add)))(),
                        waits=["c", "mmo"], inc="c")
                    p.do("vector", (lambda dti=dti: (lambda e: e.tensor_mul(
                        y3bf[:, dti * TC:(dti + 1) * TC], tp_t[:],
                        zsl[:, dti * TC:(dti + 1) * TC])))(), waits=["c"], inc="c")
                for ts in range(TC // 128):
                    for mh in range(2):
                        for kc in range(16):
                            p.do("tensor", (lambda ts=ts, mh=mh, kc=kc: (lambda e: nc.tensor.matmul(
                                ps_o[mh][:],
                                y3bf[:, kc * TC + ts * 128: kc * TC + ts * 128 + 128],
                                opT[:, kc * 1024 + mh * 512: kc * 1024 + mh * 512 + 512],
                                start=(kc == 0), stop=(kc == 15))))(),
                                waits=(["c", "mmoF"] if kc == 0 else []),
                                inc=("mmo" if kc == 15 else None))
                        p.do("vector", (lambda ts=ts, mh=mh: (lambda e: e.tensor_add(
                            x_tm[ts][:, mh * 512:(mh + 1) * 512], ps_o[mh][:],
                            x_tm[ts][:, mh * 512:(mh + 1) * 512])))(),
                            waits=["mmo", ("st", p.cur("st"))], inc="mmoF")
                    p.dma("gpsimd", y_d[row0 + ts * 128: row0 + (ts + 1) * 128, :], x_tm[ts][:],
                          waits=[("mmoF", p.cur("mmoF"))], inc="st")
        p.do("gpsimd", lambda e: e.engine_nop(), waits=["st"], inc=None)
        p.emit(nc, ctx)
    return nc


def prep_mamba_weights(d):
    out = {}
    out["ipT_dr"] = np.ascontiguousarray(d["L1_attn_in_proj_w"].T)
    out["opT_dr"] = np.ascontiguousarray(d["L1_attn_out_proj_w"].T)
    out["xpT_dr"] = np.ascontiguousarray(d["L1_attn_x_proj_w"].T)
    out["dpT_dr"] = np.ascontiguousarray(d["L1_attn_dt_proj_w"].T)
    out["convw_dr"] = np.ascontiguousarray(
        d["L1_attn_conv_w"].reshape(16, 128, 4).transpose(1, 0, 2).reshape(128, 64))
    out["convb_dr"] = np.ascontiguousarray(d["L1_attn_conv_b"].reshape(16, 128).T)
    out["dtb_dr"] = np.ascontiguousarray(d["L1_attn_dt_proj_b"].reshape(16, 128).T)
    out["alog_dr"] = np.ascontiguousarray(
        d["L1_attn_A_log"].reshape(16, 128, 16).transpose(1, 0, 2).reshape(128, 256))
    out["dvec_dr"] = np.ascontiguousarray(d["L1_attn_D"].reshape(16, 128).T)
    out["nw_dr"] = np.ascontiguousarray(d["L1_attn_norm_weight"].reshape(8, 128).T)
    import ml_dtypes
    sel = np.zeros((32, 32, 128), np.float32)
    for n in range(32):
        sel[n, n, :] = 1.0
    out["selmat"] = np.ascontiguousarray(sel.transpose(1, 0, 2).reshape(32, 4096)).astype(ml_dtypes.bfloat16)
    out["idmat"] = np.eye(128).astype(ml_dtypes.bfloat16)
    return out





QW = 128
TB = 1024

def build_blockB(dbg=False):
    nc = bass.Bass("TRN2", target_bir_lowering=False)
    x_d = nc.dram_tensor("x", [T_ROWS, D], F32, kind="ExternalInput").ap()
    qkvw_d = nc.dram_tensor("qkvw_dr", [1024, 3072], F32, kind="ExternalInput").ap()
    qkvbc_d = nc.dram_tensor("qkvbc_dr", [128, 24], F32, kind="ExternalInput").ap()
    vbr_d = nc.dram_tensor("vbr_dr", [1, 1024], F32, kind="ExternalInput").ap()
    outw_d = nc.dram_tensor("outw_dr", [1024, 1024], F32, kind="ExternalInput").ap()
    outbr_d = nc.dram_tensor("outbr_dr", [1, 1024], F32, kind="ExternalInput").ap()
    n1_d = nc.dram_tensor("n1_dr", [128, 16], F32, kind="ExternalInput").ap()
    n2_d = nc.dram_tensor("n2_dr", [128, 16], F32, kind="ExternalInput").ap()
    w1all_d = nc.dram_tensor("w1all_dr", [9 * 1024, 2048], F32, kind="ExternalInput").ap()
    w2all_d = nc.dram_tensor("w2all_dr", [9 * 2048, 1024], F32, kind="ExternalInput").ap()
    b1c_d = nc.dram_tensor("b1c_dr", [128, 9 * 16], F32, kind="ExternalInput").ap()
    b2c_d = nc.dram_tensor("b2c_dr", [128, 9 * 8], F32, kind="ExternalInput").ap()
    centT_d = nc.dram_tensor("centT_dr", [1024, 8], F32, kind="ExternalInput").ap()
    rb_d = nc.dram_tensor("rb_dr", [1, 8], F32, kind="ExternalInput").ap()
    sel8_d = nc.dram_tensor("sel8_dr", [8, 8 * 128], BF16, kind="ExternalInput").ap()
    id_d = nc.dram_tensor("idmat", [128, 128], BF16, kind="ExternalInput").ap()
    idf_d = nc.dram_tensor("idmatf", [128, 128], F32, kind="ExternalInput").ap()
    cmask_d = nc.dram_tensor("cmaskf_dr", [128, 128], F32, kind="ExternalInput").ap()
    y_d = nc.dram_tensor("y", [T_ROWS, D], F32, kind="ExternalOutput").ap()
    if dbg:
        dbg_attn = nc.dram_tensor("dbg_attn", [2 * 128, 16384], BF16, kind="ExternalOutput").ap()
        dbg_x1 = nc.dram_tensor("dbg_x1", [2 * 128, 16384], BF16, kind="ExternalOutput").ap()
        dbg_gat = nc.dram_tensor("dbg_gat", [2 * 8, 2048], BF16, kind="ExternalOutput").ap()
        dbg_q = nc.dram_tensor("dbg_q", [2 * 128, 16384], BF16, kind="ExternalOutput").ap()
        dbg_s = nc.dram_tensor("dbg_s", [128, 2048], F32, kind="ExternalOutput").ap()
        dbg_e = nc.dram_tensor("dbg_e", [128, 2048], BF16, kind="ExternalOutput").ap()
        dbg_nm = nc.dram_tensor("dbg_nm", [128, 1], F32, kind="ExternalOutput").ap()
        dbg_acc = nc.dram_tensor("dbg_acc", [128, 8192], F32, kind="ExternalOutput").ap()
        dbg_k = nc.dram_tensor("dbg_k", [2 * 128, 16384], BF16, kind="ExternalOutput").ap()
        dbg_v = nc.dram_tensor("dbg_v", [2 * 128, 16384], BF16, kind="ExternalOutput").ap()

    p = Plan()
    ctx = ExitStack()
    with ctx:
        sb = lambda name, shp, dt=F32: ctx.enter_context(nc.sbuf_tensor(name, shp, dt))
        ps = lambda name, shp, dt=F32: ctx.enter_context(nc.psum_tensor(name, shp, dt))

        AR1 = sb("AR1", [128, 16384], BF16)
        AR2 = sb("AR2", [128, 16384], BF16)
        AR3 = sb("AR3", [128, 16384], BF16)
        AR4 = sb("AR4", [128, 16384], BF16)
        AR5 = sb("AR5", [128, 2048])
        AR6 = sb("AR6", [128, 16384], BF16)
        ebf = sb("ebf", [128, 2048], BF16)
        ACC = AR4.bitcast(F32)

        ident = sb("ident", [128, 128], BF16)
        identf = sb("identf", [128, 128], F32)
        onescf = sb("onescf", [128, 1], F32)
        sel8 = sb("sel8", [8, 8 * 128], BF16)
        qkvbc = sb("qkvbc", [128, 24])
        qb8 = sb("qb8", [128, 8])
        vbr = sb("vbr", [1, 1024], BF16)
        outbr = sb("outbr", [1, 1024], BF16)
        n1c = sb("n1c", [128, 16]); n2c = sb("n2c", [128, 16])
        b1c = sb("b1c", [128, 9 * 16]); b2c = sb("b2c", [128, 9 * 8])
        centTf = sb("centTf", [128, 64], F32)
        xTf = sb("xTf", [128, 128], F32)
        logits_all = sb("logits_all", [128, 16 * 8], F32)
        rbrow = sb("rbrow", [1, 8])
        onesc_bf = sb("onesc_bf", [128, 1], BF16)
        onesr_bf = sb("onesr_bf", [1, 128], BF16)
        ones_f = sb("ones_f", [1, 128])
        negones_f = sb("negones_f", [1, 128])
        maxcols = sb("maxcols", [128, 16])
        colmax = sb("colmax", [128, 1])
        negm_sb = sb("negm_sb", [128, 1])
        cmask = sb("cmask", [128, 128])
        m11 = sb("m11", [1, 1])
        recr = sb("recr", [1, QW])
        ctxs = sb("ctxs", [64, QW])
        xtile = sb("xtile", [128, D])
        xln = sb("xln", [128, D]); tmp = sb("tmp", [128, D])
        xbf = sb("xbf", [128, D], BF16)
        cols = (sb("c_m", [128, 1]), sb("c_v", [128, 1]), sb("c_r", [128, 1]))
        logits = sb("logits", [128, 8])
        lmax = sb("lmax", [128, 1]); lsum = sb("lsum", [128, 1])
        sim = sb("sim", [128, 8]); vb = sb("vb", [128, 8]); vb2 = sb("vb2", [128, 8])
        mk1 = sb("mk1", [128, 8]); mk2 = sb("mk2", [128, 8])
        gsum = sb("gsum", [128, 1])
        gates_bf = sb("gates_bf", [128, 8], BF16)
        gatesT = sb("gatesT", [8, S], BF16)
        gb_sb = sb("gb_sb", [128, 512])
        t1_sb = sb("t1_sb", [128, 512])
        sqs = sb("sqs", [128, 256])
        statm = sb("statm", [128, 1]); statv = sb("statv", [128, 1])
        statm2 = sb("statm2", [128, 1])

        ps_s = ps("ps_s", [128, 512])
        ps_t = ps("ps_t", [128, 128], BF16)
        ps_r = ps("ps_r", [1, 256])
        ps_c = ps("ps_c", [128, 256])
        ps_bc = ps("ps_bc", [128, 256])
        ps_h = ps("ps_h", [128, 512])
        ps_o = ps("ps_o", [128, 512])
        ps_g = ps("ps_g", [128, 512])

        p.dma("sync", ident[:], id_d, inc="ld")
        p.dma("sync", identf[:], idf_d, inc="ld")
        p.dma("sync", cmask[:], cmask_d, inc="ld")
        p.dma("sync", sel8[:], sel8_d, inc="ld")
        p.dma("sync", qkvbc[:], qkvbc_d, inc="ld")
        p.dma("sync", n1c[:], n1_d, inc="ld")
        p.dma("sync", n2c[:], n2_d, inc="ld")
        p.dma("sync", b1c[:], b1c_d, inc="ld")
        p.dma("sync", b2c[:], b2c_d, inc="ld")
        p.dma("sync", rbrow[:], rb_d, inc="ld")
        p.dma("gpsimd", vbr[:], vbr_d, inc="ld")
        p.dma("gpsimd", outbr[:], outbr_d, inc="ld")
        for dc in range(8):
            p.dma("gpsimd", centTf[:, dc * 8:(dc + 1) * 8],
                  centT_d[dc * 128:(dc + 1) * 128, :], inc="ld")
        p.do("vector", lambda e: e.memset(onesc_bf[:], 1.0), waits=[], inc="c")
        p.do("vector", lambda e: e.memset(onesr_bf[:], 1.0), waits=[], inc="c")
        p.do("vector", lambda e: e.memset(onescf[:], 1.0), waits=[], inc="c")
        p.do("vector", lambda e: e.memset(ones_f[:], 1.0), waits=[], inc="c")
        p.do("vector", lambda e: e.memset(negones_f[:], -1.0), waits=[], inc="c")
        p.do("vector", lambda e: e.tensor_scalar_mul(qb8[:], qkvbc[:, 0:8], 0.125), waits=["ld"], inc="c")

        for b in range(B_CORE):
            brow = b * S

            for tt in range(16):
                p.dma("sync", xtile[:], x_d[brow + tt * 128: brow + (tt + 1) * 128, :],
                      waits=[("c", p.cur("c")), ("mmq", p.cur("mmq")), ("stF", p.cur("stF"))], inc="ld")
                p.do("vector", lambda e: e.engine_nop(), waits=["ld"], inc="c")
                emit_layernorm(p, sb, xtile, xln, tmp, cols, "c")
                p.do("vector", lambda e: e.tensor_copy(xbf[:], xln[:]), waits=["c"], inc="c")
                for dc in range(8):
                    p.do("tensor", (lambda dc=dc: (lambda e: nc.tensor.transpose(
                        ps_t[:], xbf[:, dc * 128:(dc + 1) * 128], ident[:])))(),
                        waits=["c", "evt"], inc="tt")
                    p.do("vector", (lambda dc=dc, tt=tt: (lambda e: e.tensor_copy(
                        AR1[:, dc * 2048 + tt * 128: dc * 2048 + tt * 128 + 128], ps_t[:])))(),
                        waits=["tt"], inc="evt")

            for which in range(3):
                for dc in range(8):
                    p.dma("gpsimd", AR6[:, dc * 1024:(dc + 1) * 1024],
                          qkvw_d[dc * 128:(dc + 1) * 128, which * 1024:(which + 1) * 1024],
                          waits=[("mmq", p.cur("mmq"))], inc="ldw")
                if which < 2:
                    dst = AR2 if which == 0 else AR3
                    for mc in range(8):
                        for tb4 in range(4):
                            for dc in range(8):
                                p.do("tensor", (lambda dc=dc, mc=mc, tb4=tb4: (lambda e: nc.tensor.matmul(
                                    ps_s[:], AR6[:, dc * 1024 + mc * 128: dc * 1024 + mc * 128 + 128],
                                    AR1[:, dc * 2048 + tb4 * 512: dc * 2048 + tb4 * 512 + 512],
                                    start=(dc == 0), stop=(dc == 7))))(),
                                    waits=(["ldw", "evt", "mmq"] if dc == 0 else []),
                                    inc=("mm" if dc == 7 else None))
                            if which == 0:
                                p.do("vector", (lambda mc=mc, tb4=tb4: (lambda e: e.tensor_scalar(
                                    AR2[:, mc * 2048 + tb4 * 512: mc * 2048 + tb4 * 512 + 512],
                                    ps_s[:], 0.125, qb8[:, mc:mc + 1], ALU.mult, ALU.add)))(),
                                    waits=["mm"], inc="mmq")
                            else:
                                p.do("vector", (lambda mc=mc, tb4=tb4: (lambda e: e.tensor_scalar_add(
                                    AR3[:, mc * 2048 + tb4 * 512: mc * 2048 + tb4 * 512 + 512],
                                    ps_s[:], qkvbc[:, 8 + mc:9 + mc])))(),
                                    waits=["mm"], inc="mmq")
                else:
                    for tt in range(16):
                        for mh in range(2):
                            for dc in range(8):
                                p.do("tensor", (lambda dc=dc, tt=tt, mh=mh: (lambda e: nc.tensor.matmul(
                                    ps_s[:], AR1[:, dc * 2048 + tt * 128: dc * 2048 + tt * 128 + 128],
                                    AR6[:, dc * 1024 + mh * 512: dc * 1024 + mh * 512 + 512],
                                    start=(dc == 0), stop=False)))(),
                                    waits=(["ldw", "mmq"] if dc == 0 else []), inc=None)
                            p.do("tensor", (lambda mh=mh: (lambda e: nc.tensor.matmul(
                                ps_s[:], onesr_bf[:], vbr[:, mh * 512: mh * 512 + 512],
                                start=False, stop=True)))(), waits=[], inc="mm")
                            p.do("vector", (lambda tt=tt, mh=mh: (lambda e: e.tensor_copy(
                                AR4[:, tt * 1024 + mh * 512: tt * 1024 + mh * 512 + 512], ps_s[:])))(),
                                waits=["mm"], inc="mmq")

            for h in range(16):
                hb = (h % 2) * 64
                hc = h // 2
                for qb in range(S // QW):
                    q0 = qb * QW
                    nkt = qb + 1
                    for kt in range(nkt):
                        k0 = kt * 128
                        p.do("tensor", (lambda hb=hb, hc=hc, kt=kt, q0=q0: (lambda e: nc.tensor.matmul(
                            ps_s[:, 0:QW],
                            AR3[hb:hb + 64, hc * 2048 + kt * 128: hc * 2048 + kt * 128 + 128],
                            AR2[hb:hb + 64, hc * 2048 + q0: hc * 2048 + q0 + QW],
                            start=True, stop=True)))(),
                            waits=["c", "mmq", ("sevt", p.cur("sevt"))], inc="mm")
                        p.do("vector", (lambda kt=kt: (lambda e: e.tensor_copy(
                            AR5[:, kt * QW:(kt + 1) * QW], ps_s[:, 0:QW])))(),
                            waits=["mm"], inc="sevt")
                        if k0 + 127 >= q0:
                            p.do("vector", (lambda kt=kt: (lambda e: e.tensor_add(
                                AR5[:, kt * QW:(kt + 1) * QW],
                                AR5[:, kt * QW:(kt + 1) * QW], cmask[:])))(),
                                waits=["sevt", "ld"], inc="sevt")
                        p.do("vector", (lambda kt=kt: (lambda e: e.reduce_max(
                            maxcols[:, kt:kt + 1], AR5[:, kt * QW:(kt + 1) * QW], axis=AX.X)))(),
                            waits=["sevt"], inc="sevt")
                    p.do("vector", (lambda nkt=nkt: (lambda e: e.reduce_max(
                        colmax[:], maxcols[:, 0:nkt], axis=AX.X)))(), waits=["sevt"], inc="c")
                    p.do("tensor", lambda e: nc.tensor.transpose(
                        ps_r[0:1, 0:128], colmax[:], identf[:]), waits=["c"], inc="mmx")
                    p.do("vector", lambda e: e.reduce_max(m11[:], ps_r[0:1, 0:128], axis=AX.X),
                         waits=["mmx"], inc="c")
                    p.do("tensor", lambda e: nc.tensor.matmul(
                        ps_bc[:, 0:1], negones_f[:], m11[:], start=True, stop=True),
                        waits=["c"], inc="mmx")
                    p.do("vector", lambda e: e.tensor_copy(negm_sb[:], ps_bc[:, 0:1]), waits=["mmx"], inc="c")
                    for kt in range(nkt):
                        p.do("scalar", (lambda kt=kt: (lambda e: e.activation(
                            ebf[:, kt * QW:(kt + 1) * QW], AR5[:, kt * QW:(kt + 1) * QW],
                            AF.Exp, bias=negm_sb[:])))(), waits=["c"], inc="eexp")
                    for kt in range(nkt):
                        p.do("tensor", (lambda kt=kt: (lambda e: nc.tensor.matmul(
                            ps_r[0:1, 0:QW], onesc_bf[:], ebf[:, kt * QW:(kt + 1) * QW],
                            start=(kt == 0), stop=(kt == nkt - 1))))(),
                            waits=(["eexp"] if kt == 0 else []), inc=("mmr" if kt == nkt - 1 else None))
                    for kt in range(nkt):
                        p.do("tensor", (lambda kt=kt, h=h: (lambda e: nc.tensor.matmul(
                            ps_c[0:64, 0:QW], AR4[:, kt * 1024 + h * 64: kt * 1024 + h * 64 + 64],
                            ebf[:, kt * QW:(kt + 1) * QW],
                            start=(kt == 0), stop=(kt == nkt - 1))))(),
                            waits=(["eexp", ("cevt", p.cur("cevt"))] if kt == 0 else []),
                            inc=("mmc" if kt == nkt - 1 else None))
                    p.do("vector", lambda e: e.reciprocal(recr[:], ps_r[0:1, 0:QW]), waits=["mmr"], inc="c")
                    p.do("tensor", lambda e: nc.tensor.matmul(
                        ps_bc[:, 0:QW], ones_f[:], recr[:], start=True, stop=True),
                        waits=["c"], inc="mmx")
                    p.do("vector", lambda e: e.tensor_copy(ctxs[:], ps_c[0:64, 0:QW]), waits=["mmc"], inc="cevt")
                    p.do("vector", (lambda h=h, hb=hb, hc=hc, q0=q0: (lambda e: e.tensor_mul(
                        AR6[hb:hb + 64, hc * 2048 + q0: hc * 2048 + q0 + QW],
                        ctxs[:], ps_bc[0:64, 0:QW])))(), waits=["mmx", "cevt"], inc="c")
                    if dbg and b == 0 and h == 0 and qb == 2:
                        p.dma("gpsimd", dbg_s, AR5[:], waits=[("c", p.cur("c"))], inc="st_dbg")
                        p.dma("gpsimd", dbg_e, ebf[:], waits=[("c", p.cur("c"))], inc="st_dbg")
                        p.dma("gpsimd", dbg_nm, negm_sb[:], waits=[("c", p.cur("c"))], inc="st_dbg")

            if dbg:
                p.dma("gpsimd", dbg_attn[b * 128:(b + 1) * 128, :], AR6[:],
                      waits=[("c", p.cur("c"))], inc="st_dbg")
                p.dma("gpsimd", dbg_q[b * 128:(b + 1) * 128, :], AR2[:],
                      waits=[("c", p.cur("c"))], inc="st_dbg")
                p.dma("gpsimd", dbg_k[b * 128:(b + 1) * 128, :], AR3[:],
                      waits=[("c", p.cur("c"))], inc="st_dbg")
                p.dma("gpsimd", dbg_v[b * 128:(b + 1) * 128, :], AR4[:],
                      waits=[("c", p.cur("c"))], inc="st_dbg")
            for dc in range(8):
                p.dma("gpsimd", AR3[:, dc * 1024:(dc + 1) * 1024],
                      outw_d[dc * 128:(dc + 1) * 128, :], waits=[("c", p.cur("c"))], inc="ldw")
            for tt in range(16):
                p.dma("sync", xtile[:], x_d[brow + tt * 128: brow + (tt + 1) * 128, :],
                      waits=[("c", p.cur("c")), ("stF", p.cur("stF"))], inc="ld")
                for mh in range(2):
                    pso = ps_h if mh == 0 else ps_o
                    for dc in range(8):
                        p.do("tensor", (lambda dc=dc, tt=tt, mh=mh, pso=pso: (lambda e: nc.tensor.matmul(
                            pso[:], AR6[:, dc * 2048 + tt * 128: dc * 2048 + tt * 128 + 128],
                            AR3[:, dc * 1024 + mh * 512: dc * 1024 + mh * 512 + 512],
                            start=(dc == 0), stop=False)))(),
                            waits=(["ldw", "c", ("yevt", p.cur("yevt"))] if dc == 0 else []), inc=None)
                    p.do("tensor", (lambda mh=mh, pso=pso: (lambda e: nc.tensor.matmul(
                        pso[:], onesr_bf[:], outbr[:, mh * 512: mh * 512 + 512],
                        start=False, stop=True)))(), waits=[], inc="mm")
                p.do("vector", lambda e: e.engine_nop(), waits=["ld"], inc="c")
                emit_layernorm(p, sb, xtile, xln, tmp, cols, "c")
                p.do("vector", lambda e: e.tensor_add(xtile[:, 0:512], xln[:, 0:512], ps_h[:]), waits=["c", "mm"], inc="yevt")
                p.do("vector", lambda e: e.tensor_add(xtile[:, 512:1024], xln[:, 512:1024], ps_o[:]), waits=["c"], inc="yevt")
                emit_layernorm(p, sb, xtile, xln, tmp, cols, "c")
                p.do("vector", lambda e: e.tensor_copy(xbf[:], xln[:]), waits=["c"], inc="c")
                for dc in range(8):
                    p.do("tensor", (lambda dc=dc: (lambda e: nc.tensor.transpose(
                        ps_t[:], xbf[:, dc * 128:(dc + 1) * 128], ident[:])))(),
                        waits=["c", "evt"], inc="tt")
                    p.do("vector", (lambda dc=dc, tt=tt: (lambda e: e.tensor_scalar(
                        AR1[:, dc * 2048 + tt * 128: dc * 2048 + tt * 128 + 128], ps_t[:],
                        n1c[:, dc:dc + 1], n1c[:, 8 + dc:9 + dc], ALU.mult, ALU.add)))(),
                        waits=["tt"], inc="evt")
                for dc in range(8):
                    p.do("tensor", (lambda dc=dc: (lambda e: nc.tensor.transpose(
                        ps_s[:, 0:128], xln[:, dc * 128:(dc + 1) * 128], identf[:])))(),
                        waits=["c", "evt", ("xtF", p.cur("xtF"))], inc="mmxt")
                    p.do("vector", (lambda dc=dc: (lambda e: e.tensor_scalar(
                        xTf[:], ps_s[:, 0:128],
                        n1c[:, dc:dc + 1], n1c[:, 8 + dc:9 + dc], ALU.mult, ALU.add)))(),
                        waits=["mmxt"], inc="xtF")
                    p.do("tensor", (lambda dc=dc, tt=tt: (lambda e: nc.tensor.matmul(
                        ps_g[:, 0:8], xTf[:], centTf[:, dc * 8:(dc + 1) * 8],
                        start=(dc == 0), stop=(dc == 7))))(),
                        waits=["xtF"], inc=("mmg" if dc == 7 else "mmxt"))
                p.do("vector", (lambda tt=tt: (lambda e: e.tensor_copy(
                    logits_all[:, tt * 8:(tt + 1) * 8], ps_g[:, 0:8])))(),
                    waits=["mmg"], inc="c")

            if dbg:
                p.dma("gpsimd", dbg_x1[b * 128:(b + 1) * 128, :], AR1[:],
                      waits=[("evt", p.cur("evt"))], inc="st_dbg")
            for tt in range(16):
                p.do("vector", (lambda tt=tt: (lambda e: e.tensor_copy(
                    logits[:], logits_all[:, tt * 8:(tt + 1) * 8])))(), waits=["c"], inc="c")
                p.do("vector", lambda e: e.reduce_max(lmax[:], logits[:], axis=AX.X, negate=True), waits=["c"], inc="c")
                p.do("scalar", lambda e: e.activation(sim[:], logits[:], AF.Exp, bias=lmax[:]), waits=["c"], inc="c")
                p.do("vector", lambda e: e.reduce_sum(lsum[:], sim[:], axis=AX.X), waits=["c"], inc="c")
                p.do("vector", lambda e: e.reciprocal(lsum[:], lsum[:]), waits=["c"], inc="c")
                p.do("vector", lambda e: e.tensor_scalar_mul(sim[:], sim[:], lsum[:]), waits=["c"], inc="c")
                p.do("tensor", lambda e: nc.tensor.matmul(
                    ps_g[:, 8:16], ones_f[:], rbrow[:], start=True, stop=True),
                    waits=["c"], inc="mmg")
                p.do("vector", lambda e: e.tensor_add(vb[:], sim[:], ps_g[:, 8:16]), waits=["mmg"], inc="c")
                p.do("vector", lambda e: e.reduce_max(lmax[:], vb[:], axis=AX.X), waits=["c"], inc="c")
                p.do("vector", lambda e: e.tensor_single_scalar(mk1[:], vb[:], lmax[:], ALU.is_equal), waits=["c"], inc="c")
                p.do("vector", lambda e: e.scalar_tensor_tensor(vb2[:], mk1[:], -1e30, vb[:], ALU.mult, ALU.add), waits=["c"], inc="c")
                p.do("vector", lambda e: e.reduce_max(lmax[:], vb2[:], axis=AX.X), waits=["c"], inc="c")
                p.do("vector", lambda e: e.tensor_single_scalar(mk2[:], vb2[:], lmax[:], ALU.is_equal), waits=["c"], inc="c")
                p.do("vector", lambda e: e.tensor_add(mk1[:], mk1[:], mk2[:]), waits=["c"], inc="c")
                p.do("vector", lambda e: e.tensor_mul(sim[:], sim[:], mk1[:]), waits=["c"], inc="c")
                p.do("vector", lambda e: e.reduce_sum(gsum[:], sim[:], axis=AX.X), waits=["c"], inc="c")
                p.do("vector", lambda e: e.reciprocal(gsum[:], gsum[:]), waits=["c"], inc="c")
                p.do("vector", lambda e: e.tensor_scalar_mul(gates_bf[:], sim[:], gsum[:]), waits=["c"], inc="c")
                p.do("tensor", lambda e: nc.tensor.transpose(ps_t[0:8, :], gates_bf[:], ident[:]),
                     waits=["c", ("gevt", p.cur("gevt"))], inc="mmg")
                p.do("vector", (lambda tt=tt: (lambda e: e.tensor_copy(
                    gatesT[:, tt * 128:(tt + 1) * 128], ps_t[0:8, :])))(), waits=["mmg"], inc="gevt")

            if dbg:
                p.dma("gpsimd", dbg_gat[b * 8:(b + 1) * 8, :], gatesT[:],
                      waits=[("gevt", p.cur("gevt"))], inc="st_dbg")
            for tb in range(S // TB):
                t0 = tb * TB
                for mc in range(8):
                    for th in range(2):
                        p.do("vector", (lambda mc=mc, th=th, t0=t0: (lambda e: e.tensor_copy(
                            ACC[:, mc * 1024 + th * 512: mc * 1024 + th * 512 + 512],
                            AR1[:, mc * 2048 + t0 + th * 512: mc * 2048 + t0 + th * 512 + 512])))(),
                            waits=[("c", p.cur("c")), ("mmo2", p.cur("mmo2"))], inc="c")
                for ex in range(9):
                    for dc in range(8):
                        p.dma("gpsimd", AR3[:, dc * 2048:(dc + 1) * 2048],
                              w1all_d[ex * 1024 + dc * 128: ex * 1024 + dc * 128 + 128, :],
                              waits=[("mmh", p.cur("mmh")), ("mmo2", p.cur("mmo2")),
                                     ("yevt", p.cur("yevt"))], inc="ldw")
                    for ffc in range(16):
                        for th in range(2):
                            for dc in range(8):
                                p.do("tensor", (lambda dc=dc, ffc=ffc, th=th, t0=t0: (lambda e: nc.tensor.matmul(
                                    ps_h[:], AR3[:, dc * 2048 + ffc * 128: dc * 2048 + ffc * 128 + 128],
                                    AR1[:, dc * 2048 + t0 + th * 512: dc * 2048 + t0 + th * 512 + 512],
                                    start=(dc == 0), stop=(dc == 7))))(),
                                    waits=(["ldw", ("hevt", p.cur("hevt")),
                                            ("mmo2", p.cur("mmo2")),
                                            ("statF", p.cur("statF"))] if dc == 0 else []),
                                    inc=("mmh" if dc == 7 else None))
                            p.do("scalar", (lambda ffc=ffc, th=th, ex=ex: (lambda e: e.activation(
                                AR2[:, ffc * 1024 + th * 512: ffc * 1024 + th * 512 + 512],
                                ps_h[:], AF.Gelu, bias=b1c[:, ex * 16 + ffc: ex * 16 + ffc + 1])))(),
                                waits=["mmh"], inc="hevt")
                    for fc in range(16):
                        p.dma("gpsimd", AR3[:, fc * 1024:(fc + 1) * 1024],
                              w2all_d[ex * 2048 + fc * 128: ex * 2048 + fc * 128 + 128, :],
                              waits=[("mmh", p.cur("mmh")), ("mmo2", p.cur("mmo2"))], inc="ldw")
                    for th in range(2):
                        if ex > 0:
                            p.do("tensor", (lambda ex=ex, t0=t0, th=th: (lambda e: nc.tensor.matmul(
                                ps_g[:], sel8[:, (ex - 1) * 128: ex * 128],
                                gatesT[:, t0 + th * 512: t0 + th * 512 + 512],
                                start=True, stop=True)))(),
                                waits=["gevt", "hevt", ("gbF", p.cur("gbF"))], inc="mmg2")
                            p.do("vector", lambda e: e.tensor_copy(gb_sb[:], ps_g[:]), waits=["mmg2"], inc="gbF")
                        for mc in range(8):
                            for fc in range(16):
                                p.do("tensor", (lambda fc=fc, mc=mc, th=th: (lambda e: nc.tensor.matmul(
                                    ps_o[:], AR3[:, fc * 1024 + mc * 128: fc * 1024 + mc * 128 + 128],
                                    AR2[:, fc * 1024 + th * 512: fc * 1024 + th * 512 + 512],
                                    start=(fc == 0), stop=(fc == 15))))(),
                                    waits=(["ldw", "hevt", ("accF", p.cur("accF"))] if fc == 0 else []),
                                    inc=("mmo2" if fc == 15 else None))
                            if ex == 0:
                                p.do("vector", (lambda mc=mc, th=th, ex=ex: (lambda e: e.scalar_tensor_tensor(
                                    ACC[:, mc * 1024 + th * 512: mc * 1024 + th * 512 + 512],
                                    ps_o[:], b2c[:, ex * 8 + mc: ex * 8 + mc + 1],
                                    ACC[:, mc * 1024 + th * 512: mc * 1024 + th * 512 + 512],
                                    ALU.add, ALU.add)))(), waits=["mmo2"], inc="accF")
                            else:
                                p.do("vector", (lambda mc=mc, th=th, ex=ex: (lambda e: e.scalar_tensor_tensor(
                                    t1_sb[:], ps_o[:], b2c[:, ex * 8 + mc: ex * 8 + mc + 1],
                                    gb_sb[:], ALU.add, ALU.mult)))(), waits=["mmo2", "gbF"], inc="c")
                                p.do("vector", (lambda mc=mc, th=th: (lambda e: e.tensor_add(
                                    ACC[:, mc * 1024 + th * 512: mc * 1024 + th * 512 + 512],
                                    ACC[:, mc * 1024 + th * 512: mc * 1024 + th * 512 + 512],
                                    t1_sb[:])))(), waits=["c"], inc="accF")
                if dbg and b == 0 and tb == 1:
                    p.dma("gpsimd", dbg_acc, ACC[:], waits=[("accF", p.cur("accF"))], inc="st_dbg")
                for ts2 in range(TB // 128):
                    tg = t0 + ts2 * 128
                    for mc in range(8):
                        p.do("scalar", (lambda mc=mc, ts2=ts2: (lambda e: e.activation(
                            sqs[:, 0:128], ACC[:, mc * 1024 + ts2 * 128: mc * 1024 + ts2 * 128 + 128],
                            AF.Square)))(), waits=["accF", ("c", p.cur("c")), ("mmsq", p.cur("mmsq"))], inc="c")
                        p.do("tensor", (lambda mc=mc, ts2=ts2: (lambda e: nc.tensor.matmul(
                            ps_c[:, 0:1], ACC[:, mc * 1024 + ts2 * 128: mc * 1024 + ts2 * 128 + 128],
                            onescf[:], start=(mc == 0), stop=(mc == 7))))(),
                            waits=(["c", ("statF", p.cur("statF"))] if mc == 0 else []),
                            inc=("mmst" if mc == 7 else None))
                        p.do("tensor", (lambda mc=mc: (lambda e: nc.tensor.matmul(
                            ps_h[:, 0:1], sqs[:, 0:128], onescf[:], start=(mc == 0), stop=(mc == 7))))(),
                            waits=["c"], inc="mmsq")
                    p.do("vector", lambda e: e.tensor_scalar_mul(statm[:], ps_c[:, 0:1], 1.0 / 1024), waits=["mmst"], inc="c")
                    p.do("vector", lambda e: e.tensor_mul(statm2[:], statm[:], statm[:]), waits=["c"], inc="c")
                    p.do("vector", lambda e: e.tensor_scalar_mul(statv[:], ps_h[:, 0:1], 1.0 / 1024), waits=["c", ("mmsq", p.cur("mmsq"))], inc="statF")
                    p.do("vector", lambda e: e.tensor_sub(statv[:], statv[:], statm2[:]), waits=["c", "statF"], inc="c")
                    p.do("vector", lambda e: e.tensor_scalar_add(statv[:], statv[:], 1e-5), waits=["c"], inc="c")
                    p.do("scalar", lambda e: e.activation(statv[:], statv[:], AF.Sqrt), waits=["c"], inc="c")
                    p.do("vector", lambda e: e.reciprocal(statv[:], statv[:]), waits=["c"], inc="c")
                    p.do("tensor", lambda e: nc.tensor.transpose(ps_r[0:1, 0:128], statm[:], identf[:]),
                         waits=["c"], inc="mmst")
                    p.do("vector", lambda e: e.tensor_copy(ctxs[0:1, 0:128], ps_r[0:1, 0:128]), waits=["mmst"], inc="c")
                    p.do("tensor", lambda e: nc.tensor.transpose(ps_r[0:1, 0:128], statv[:], identf[:]),
                         waits=["c"], inc="mmst")
                    p.do("vector", lambda e: e.tensor_copy(recr[0:1, 0:128], ps_r[0:1, 0:128]), waits=["mmst"], inc="c")
                    p.do("tensor", lambda e: nc.tensor.matmul(
                        ps_bc[:, 0:128], ones_f[:], ctxs[0:1, 0:128], start=True, stop=True), waits=["c"], inc="mmst")
                    p.do("tensor", lambda e: nc.tensor.matmul(
                        ps_bc[:, 128:256], ones_f[:], recr[0:1, 0:128], start=True, stop=True), waits=["c"], inc="mmst")
                    p.dma("sync", xtile[:], x_d[brow + tg: brow + tg + 128, :],
                          waits=[("c", p.cur("c")), ("stF", p.cur("stF"))], inc="ld")
                    for mc in range(8):
                        p.do("vector", (lambda mc=mc, ts2=ts2: (lambda e: e.tensor_sub(
                            sqs[:, 0:128], ACC[:, mc * 1024 + ts2 * 128: mc * 1024 + ts2 * 128 + 128],
                            ps_bc[:, 0:128])))(), waits=["mmst", "c", ("c", p.cur("c"))], inc="c")
                        p.do("vector", lambda e: e.tensor_mul(sqs[:, 128:256], sqs[:, 0:128], ps_bc[:, 128:256]), waits=["c"], inc="c")
                        p.do("vector", (lambda mc=mc: (lambda e: e.tensor_scalar(
                            xbf[:, 0:128], sqs[:, 128:256], n2c[:, mc:mc + 1], n2c[:, 8 + mc:9 + mc],
                            ALU.mult, ALU.add)))(), waits=["c"], inc="c")
                        p.do("tensor", lambda e: nc.tensor.transpose(ps_t[:], xbf[:, 0:128], ident[:]),
                             waits=["c", ("oevt", p.cur("oevt"))], inc="mmst")
                        p.do("vector", (lambda mc=mc: (lambda e: e.tensor_add(
                            xtile[:, mc * 128:(mc + 1) * 128], ps_t[:], xtile[:, mc * 128:(mc + 1) * 128])))(),
                            waits=["mmst", "ld"], inc="oevt")
                    p.dma("gpsimd", y_d[brow + tg: brow + tg + 128, :], xtile[:],
                          waits=[("oevt", p.cur("oevt"))], inc="stF")
        p.do("gpsimd", lambda e: e.engine_nop(), waits=["stF"], inc=None)
        p.emit(nc, ctx)
    return nc


def prep_ffn_weights(d):
    import ml_dtypes
    out = {}
    out["qkvw_dr"] = np.ascontiguousarray(d["L0_ffn_qkv_w"])
    out["qkvbc_dr"] = np.ascontiguousarray(d["L0_ffn_qkv_b"].reshape(24, 128).T)
    out["vbr_dr"] = np.ascontiguousarray(d["L0_ffn_qkv_b"][2048:].reshape(1, 1024))
    out["outw_dr"] = np.ascontiguousarray(d["L0_ffn_out_w"])
    out["outbr_dr"] = np.ascontiguousarray(d["L0_ffn_out_b"].reshape(1, 1024))
    out["n1_dr"] = np.ascontiguousarray(np.concatenate(
        [d["L0_ffn_norm1_w"].reshape(8, 128).T, d["L0_ffn_norm1_b"].reshape(8, 128).T], axis=1))
    out["n2_dr"] = np.ascontiguousarray(np.concatenate(
        [d["L0_ffn_norm2_w"].reshape(8, 128).T, d["L0_ffn_norm2_b"].reshape(8, 128).T], axis=1))
    w1 = np.concatenate([d["L0_ffn_sw1"], d["L0_ffn_rw1"]], axis=0)
    w2 = np.concatenate([d["L0_ffn_sw2"], d["L0_ffn_rw2"]], axis=0)
    b1 = np.concatenate([d["L0_ffn_sb1"], d["L0_ffn_rb1"]], axis=0)
    b2 = np.concatenate([d["L0_ffn_sb2"], d["L0_ffn_rb2"]], axis=0)
    out["w1all_dr"] = np.ascontiguousarray(w1.reshape(9 * 1024, 2048))
    out["w2all_dr"] = np.ascontiguousarray(w2.reshape(9 * 2048, 1024))
    out["b1c_dr"] = np.ascontiguousarray(b1.reshape(9, 16, 128).transpose(2, 0, 1).reshape(128, 144))
    out["b2c_dr"] = np.ascontiguousarray(b2.reshape(9, 8, 128).transpose(2, 0, 1).reshape(128, 72))
    out["centT_dr"] = np.ascontiguousarray(d["L0_ffn_centroid"].T)
    out["rb_dr"] = np.ascontiguousarray(d["L0_ffn_route_bias"].reshape(1, 8))
    sel = np.zeros((8, 8, 128), np.float32)
    for n in range(8):
        sel[n, n, :] = 1.0
    out["sel8_dr"] = np.ascontiguousarray(sel.transpose(1, 0, 2).reshape(8, 1024)).astype(ml_dtypes.bfloat16)
    out["idmat"] = np.eye(128).astype(ml_dtypes.bfloat16)
    ii = np.arange(128)[:, None]; jj = np.arange(128)[None, :]
    out["cmaskf_dr"] = np.where(jj >= ii, 0.0, -1e30).astype(np.float32)
    return out





def build_blockA():
    nc = bass.Bass("TRN2", target_bir_lowering=False)
    x_d = nc.dram_tensor("x", [T_ROWS, D], F32, kind="ExternalInput").ap()
    rel_d = nc.dram_tensor("rel_dr", [1024, 1024], F32, kind="ExternalInput").ap()
    wq_d = nc.dram_tensor("wq_dr", [1024, 1024], F32, kind="ExternalInput").ap()
    wk_d = nc.dram_tensor("wk_dr", [1024, 1024], F32, kind="ExternalInput").ap()
    wv_d = nc.dram_tensor("wv_dr", [1024, 1024], F32, kind="ExternalInput").ap()
    wpk_d = nc.dram_tensor("wpk_dr", [1024, 1024], F32, kind="ExternalInput").ap()
    wpq_d = nc.dram_tensor("wpq_dr", [1024, 1024], F32, kind="ExternalInput").ap()
    bqkv_d = nc.dram_tensor("bqkv_dr", [128, 24], F32, kind="ExternalInput").ap()
    bvr_d = nc.dram_tensor("bvr_dr", [1, 1024], F32, kind="ExternalInput").ap()
    bpq_d = nc.dram_tensor("bpq_dr", [128, 8], F32, kind="ExternalInput").ap()
    id_d = nc.dram_tensor("idmat", [128, 128], BF16, kind="ExternalInput").ap()
    idf_d = nc.dram_tensor("idmatf", [128, 128], F32, kind="ExternalInput").ap()
    mge_d = nc.dram_tensor("mge_dr", [128, 128], F32, kind="ExternalInput").ap()
    mlt_d = nc.dram_tensor("mlt_dr", [128, 128], F32, kind="ExternalInput").ap()
    y_d = nc.dram_tensor("y", [T_ROWS, D], F32, kind="ExternalOutput").ap()

    poskD = nc.dram_tensor("poskD", [1024, 1024], BF16, kind="Internal").ap()
    posqD = nc.dram_tensor("posqD", [1024, 1024], BF16, kind="Internal").ap()
    c2pR = nc.dram_tensor("c2pR", [2052, 1024], BF16, kind="Internal").ap()
    p2cA = nc.dram_tensor("p2cA", [2052, 1024], BF16, kind="Internal").ap()

    p = Plan()
    ctx = ExitStack()
    with ctx:
        sb = lambda name, shp, dt=F32: ctx.enter_context(nc.sbuf_tensor(name, shp, dt))
        ps = lambda name, shp, dt=F32: ctx.enter_context(nc.psum_tensor(name, shp, dt))

        AR1 = sb("AR1", [128, 16384], BF16)
        AR2 = sb("AR2", [128, 16384], BF16)
        AR3 = sb("AR3", [128, 16384], BF16)
        AR4 = sb("AR4", [128, 16384], BF16)
        AR6 = sb("AR6", [128, 16384], BF16)
        AR5 = sb("AR5", [128, 2048])
        ebf = sb("ebf", [128, 2048], BF16)
        c2pr_sb = sb("c2pr_sb", [128, 512], BF16)
        p2ca_sb = sb("p2ca_sb", [128, 512], BF16)
        poskh = sb("poskh", [128, 1024], BF16)
        posqh = sb("posqh", [128, 1024], BF16)
        ident = sb("ident", [128, 128], BF16)
        identf = sb("identf", [128, 128], F32)
        mge = sb("mge", [128, 128], F32)
        mlt = sb("mlt", [128, 128], F32)
        bqkv = sb("bqkv", [128, 24])
        qb8 = sb("qb8", [128, 8])
        bvr = sb("bvr", [1, 1024], BF16)
        bpqc = sb("bpqc", [128, 8])
        onesc_bf = sb("onesc_bf", [128, 1], BF16)
        onesr_bf = sb("onesr_bf", [1, 128], BF16)
        ones_f = sb("ones_f", [1, 128])
        negones_f = sb("negones_f", [1, 128])
        maxcols = sb("maxcols", [128, 16])
        colmax = sb("colmax", [128, 1])
        negm_sb = sb("negm_sb", [128, 1])
        m11 = sb("m11", [1, 1])
        recr = sb("recr", [1, 128])
        ctxs = sb("ctxs", [64, 128])
        xtile = sb("xtile", [128, D])
        xln = sb("xln", [128, D]); tmp = sb("tmp", [128, D])
        xbf = sb("xbf", [128, D], BF16)
        cols = (sb("c_m", [128, 1]), sb("c_v", [128, 1]), sb("c_r", [128, 1]))
        pconsts = sb("pconsts", [128, 32])
        cc_lo = sb("cc_lo", [128, 128]); cc_hi = sb("cc_hi", [128, 128])
        g_t = sb("g_t", [128, 128]); g2_t = sb("g2_t", [128, 128])
        t1_t = sb("t1_t", [128, 128])
        cg_bf = sb("cg_bf", [128, 128], BF16)
        pg_bf = sb("pg_bf", [128, 128], BF16)

        ps_s = ps("ps_s", [128, 512])
        ps_t = ps("ps_t", [128, 128], BF16)
        ps_r = ps("ps_r", [1, 256])
        ps_c = ps("ps_c", [128, 256])
        ps_bc = ps("ps_bc", [128, 256])

        SQ192 = float(1.0 / (8.0 * np.sqrt(np.float32(192.0))))

        for dst, srcd in [(ident, id_d), (identf, idf_d), (mge, mge_d), (mlt, mlt_d),
                          (bqkv, bqkv_d), (bpqc, bpq_d)]:
            p.dma("sync", dst[:], srcd, inc="ld")
        p.dma("gpsimd", bvr[:], bvr_d, inc="ldw")
        p.do("vector", lambda e: e.memset(onesc_bf[:], 1.0), inc="c")
        p.do("vector", lambda e: e.memset(onesr_bf[:], 1.0), inc="c")
        p.do("vector", lambda e: e.memset(ones_f[:], 1.0), inc="c")
        p.do("vector", lambda e: e.memset(negones_f[:], -1.0), inc="c")
        p.do("vector", lambda e: e.memset(ebf[0:2, 0:1024], 0.0), inc="c")
        p.do("vector", lambda e: e.tensor_scalar_mul(qb8[:], bqkv[:, 0:8], 0.125), waits=["ld"], inc="c")
        for dd in (c2pR, p2cA):
            p.dma("gpsimd", dd[0:2, :], ebf[0:2, 0:1024], waits=[("c", p.cur("c"))], inc="ldw")
            p.dma("gpsimd", dd[2050:2052, :], ebf[0:2, 0:1024], waits=[("c", p.cur("c"))], inc="ldw")

        for rt in range(8):
            p.dma("sync", xtile[:], rel_d[rt * 128:(rt + 1) * 128, :],
                  waits=[("c", p.cur("c"))], inc="ld")
            p.do("vector", lambda e: e.engine_nop(), waits=["ld"], inc="c")
            p.do("vector", lambda e: e.tensor_copy(xbf[:], xtile[:]), waits=["c"], inc="c")
            for ec in range(8):
                p.do("tensor", (lambda ec=ec: (lambda e: nc.tensor.transpose(
                    ps_t[:], xbf[:, ec * 128:(ec + 1) * 128], ident[:])))(),
                    waits=["c", "evt"], inc="tt")
                p.do("vector", (lambda ec=ec, rt=rt: (lambda e: e.tensor_copy(
                    AR6[:, ec * 1024 + rt * 128: ec * 1024 + rt * 128 + 128], ps_t[:])))(),
                    waits=["tt"], inc="evt")
        for scaled, wd, dst in [(False, wpk_d, poskD), (True, wpq_d, posqD)]:
            for dc in range(8):
                p.dma("gpsimd", AR6[:, 8192 + dc * 1024: 8192 + (dc + 1) * 1024],
                      wd[dc * 128:(dc + 1) * 128, :], waits=[("mmw", p.cur("mmw"))], inc="ldw")
            for mc in range(8):
                for rb in range(2):
                    for ec in range(8):
                        p.do("tensor", (lambda ec=ec, mc=mc, rb=rb: (lambda e: nc.tensor.matmul(
                            ps_s[:], AR6[:, 8192 + ec * 1024 + mc * 128: 8192 + ec * 1024 + mc * 128 + 128],
                            AR6[:, ec * 1024 + rb * 512: ec * 1024 + rb * 512 + 512],
                            start=(ec == 0), stop=(ec == 7))))(),
                            waits=(["ldw", "evt", ("posF", p.cur("posF"))] if ec == 0 else []),
                            inc=("mmw" if ec == 7 else None))
                    if scaled:
                        p.do("vector", (lambda mc=mc: (lambda e: e.tensor_scalar(
                            c2pr_sb[:, 0:512], ps_s[:], bpqc[:, mc:mc + 1], SQ192,
                            ALU.add, ALU.mult)))(), waits=["mmw"], inc="posF")
                    else:
                        p.do("vector", lambda e: e.tensor_copy(c2pr_sb[:, 0:512], ps_s[:]),
                             waits=["mmw"], inc="posF")
                    p.dma("gpsimd", dst[mc * 128:(mc + 1) * 128, rb * 512:(rb + 1) * 512],
                          c2pr_sb[:, 0:512], waits=[("posF", p.cur("posF"))], inc="stpos")

        for b in range(B_CORE):
            brow = b * S
            for tt in range(16):
                p.dma("sync", xtile[:], x_d[brow + tt * 128: brow + (tt + 1) * 128, :],
                      waits=[("c", p.cur("c")), ("mmq", p.cur("mmq")), ("stF", p.cur("stF"))], inc="ld")
                p.do("vector", lambda e: e.engine_nop(), waits=["ld"], inc="c")
                emit_layernorm(p, sb, xtile, xln, tmp, cols, "c")
                p.do("vector", lambda e: e.tensor_copy(xbf[:], xln[:]), waits=["c"], inc="c")
                for dc in range(8):
                    p.do("tensor", (lambda dc=dc: (lambda e: nc.tensor.transpose(
                        ps_t[:], xbf[:, dc * 128:(dc + 1) * 128], ident[:])))(),
                        waits=["c", "evt"], inc="tt")
                    p.do("vector", (lambda dc=dc, tt=tt: (lambda e: e.tensor_copy(
                        AR1[:, dc * 2048 + tt * 128: dc * 2048 + tt * 128 + 128], ps_t[:])))(),
                        waits=["tt"], inc="evt")
            for which, wd in [(0, wq_d), (1, wk_d), (2, wv_d)]:
                for dc in range(8):
                    p.dma("gpsimd", AR6[:, 8192 + dc * 1024: 8192 + (dc + 1) * 1024],
                          wd[dc * 128:(dc + 1) * 128, :],
                          waits=[("mmq", p.cur("mmq")), ("mmw", p.cur("mmw"))], inc="ldw")
                if which < 2:
                    for mc in range(8):
                        for tb4 in range(4):
                            for dc in range(8):
                                p.do("tensor", (lambda dc=dc, mc=mc, tb4=tb4: (lambda e: nc.tensor.matmul(
                                    ps_s[:], AR6[:, 8192 + dc * 1024 + mc * 128: 8192 + dc * 1024 + mc * 128 + 128],
                                    AR1[:, dc * 2048 + tb4 * 512: dc * 2048 + tb4 * 512 + 512],
                                    start=(dc == 0), stop=(dc == 7))))(),
                                    waits=(["ldw", "evt", "mmq"] if dc == 0 else []),
                                    inc=("mm" if dc == 7 else None))
                            if which == 0:
                                p.do("vector", (lambda mc=mc, tb4=tb4: (lambda e: e.tensor_scalar(
                                    AR2[:, mc * 2048 + tb4 * 512: mc * 2048 + tb4 * 512 + 512],
                                    ps_s[:], 0.125, qb8[:, mc:mc + 1], ALU.mult, ALU.add)))(),
                                    waits=["mm"], inc="mmq")
                            else:
                                p.do("vector", (lambda mc=mc, tb4=tb4: (lambda e: e.tensor_scalar_add(
                                    AR3[:, mc * 2048 + tb4 * 512: mc * 2048 + tb4 * 512 + 512],
                                    ps_s[:], bqkv[:, 8 + mc:9 + mc])))(),
                                    waits=["mm"], inc="mmq")
                else:
                    for tt in range(16):
                        for mh in range(2):
                            for dc in range(8):
                                p.do("tensor", (lambda dc=dc, tt=tt, mh=mh: (lambda e: nc.tensor.matmul(
                                    ps_s[:], AR1[:, dc * 2048 + tt * 128: dc * 2048 + tt * 128 + 128],
                                    AR6[:, 8192 + dc * 1024 + mh * 512: 8192 + dc * 1024 + mh * 512 + 512],
                                    start=(dc == 0), stop=False)))(),
                                    waits=(["ldw", "mmq"] if dc == 0 else []), inc=None)
                            p.do("tensor", (lambda mh=mh: (lambda e: nc.tensor.matmul(
                                ps_s[:], onesr_bf[:], bvr[:, mh * 512: mh * 512 + 512],
                                start=False, stop=True)))(),
                                waits=(["ldw"] if tt == 0 and mh == 0 else []), inc="mm")
                            p.do("vector", (lambda tt=tt, mh=mh: (lambda e: e.tensor_copy(
                                AR4[:, tt * 1024 + mh * 512: tt * 1024 + mh * 512 + 512], ps_s[:])))(),
                                waits=["mm"], inc="mmq")

            for h in range(16):
                hb = (h % 2) * 64
                hc = h // 2
                p.dma("sync", poskh[hb:hb + 64, :], poskD[h * 64:(h + 1) * 64, :],
                      waits=[("cF", p.cur("cF")), ("stpos", p.cur("stpos"))], inc="ldp")
                p.dma("sync", posqh[hb:hb + 64, :], posqD[h * 64:(h + 1) * 64, :],
                      waits=[("cF", p.cur("cF")), ("stpos", p.cur("stpos"))], inc="ldp")
                for qt in range(16):
                    for rb in range(2):
                        p.do("tensor", (lambda hb=hb, hc=hc, qt=qt, rb=rb: (lambda e: nc.tensor.matmul(
                            ps_s[:], AR2[hb:hb + 64, hc * 2048 + qt * 128: hc * 2048 + qt * 128 + 128],
                            poskh[hb:hb + 64, rb * 512:(rb + 1) * 512],
                            start=True, stop=True)))(),
                            waits=(["ldp", "mmq", ("aevt", p.cur("aevt"))] if rb == 0 else ["aevt"]),
                            inc="mma")
                        p.do("vector", (lambda rb=rb: (lambda e: e.tensor_copy(
                            c2pr_sb[:], ps_s[:, 511::-1])))(),
                            waits=["mma", ("sta", p.cur("sta"))], inc="aevt")
                        p.dma("gpsimd", c2pR[2 + qt * 128: 2 + qt * 128 + 128,
                                             (1 - rb) * 512:(2 - rb) * 512],
                              c2pr_sb[:], waits=[("aevt", p.cur("aevt"))], inc="sta")
                for kt in range(16):
                    for rb in range(2):
                        p.do("tensor", (lambda hb=hb, hc=hc, kt=kt, rb=rb: (lambda e: nc.tensor.matmul(
                            ps_s[:], AR3[hb:hb + 64, hc * 2048 + kt * 128: hc * 2048 + kt * 128 + 128],
                            posqh[hb:hb + 64, rb * 512:(rb + 1) * 512],
                            start=True, stop=True)))(),
                            waits=["aevt"], inc="mma")
                        p.do("vector", (lambda rb=rb: (lambda e: e.tensor_copy(
                            p2ca_sb[:], ps_s[:])))(),
                            waits=["mma", ("sta", p.cur("sta"))], inc="aevt")
                        p.dma("gpsimd", p2cA[2 + kt * 128: 2 + kt * 128 + 128,
                                             rb * 512:(rb + 1) * 512],
                              p2ca_sb[:], waits=[("aevt", p.cur("aevt"))], inc="sta")
                for kt in range(16):
                    rhs2 = bass.AP(posqh[:].tensor, hb * 1024, [[1024, 64], [1023, 2]])
                    p.do("tensor", (lambda hb=hb, hc=hc, kt=kt, rhs2=rhs2: (lambda e: nc.tensor.matmul(
                        ps_c[:, 0:2], AR3[hb:hb + 64, hc * 2048 + kt * 128: hc * 2048 + kt * 128 + 128],
                        rhs2, start=True, stop=True)))(),
                        waits=["ldp", "mmq", ("cF", p.cur("cF"))], inc="mma")
                    p.do("vector", (lambda kt=kt: (lambda e: e.tensor_copy(
                        pconsts[:, kt * 2: kt * 2 + 2], ps_c[:, 0:2])))(), waits=["mma"], inc="cF")

                for qb in range(16):
                    q0 = qb * 128
                    for sel, cct in [(0, cc_lo), (1023, cc_hi)]:
                        p.do("tensor", (lambda hb=hb, hc=hc, q0=q0, sel=sel: (lambda e: nc.tensor.matmul(
                            ps_r[0:1, 0:128], poskh[hb:hb + 64, sel:sel + 1],
                            AR2[hb:hb + 64, hc * 2048 + q0: hc * 2048 + q0 + 128],
                            start=True, stop=True)))(),
                            waits=[("cF", p.cur("cF")), "mmq"], inc="mma")
                        p.do("vector", lambda e: e.tensor_copy(recr[:], ps_r[0:1, 0:128]),
                             waits=["mma"], inc="cF")
                        p.do("tensor", lambda e: nc.tensor.matmul(
                            ps_bc[:, 0:128], ones_f[:], recr[:], start=True, stop=True),
                            waits=["cF"], inc="mma")
                        p.do("vector", (lambda cct=cct: (lambda e: e.tensor_copy(
                            cct[:], ps_bc[:, 0:128])))(), waits=["mma"], inc="cF")
                    for kt in range(16):
                        J00 = (qb - kt) * 128 + 512
                        k0 = kt * 128
                        p.do("tensor", (lambda hb=hb, hc=hc, kt=kt, q0=q0: (lambda e: nc.tensor.matmul(
                            ps_s[:, 0:128],
                            AR3[hb:hb + 64, hc * 2048 + kt * 128: hc * 2048 + kt * 128 + 128],
                            AR2[hb:hb + 64, hc * 2048 + q0: hc * 2048 + q0 + 128],
                            start=True, stop=True)))(),
                            waits=[("sevt", p.cur("sevt")), "cF"], inc="mms")
                        if -128 < J00 < 1152:
                            c2off = (q0 + 2) * 1024 + 511 - q0 + k0
                            c2view = bass.AP(c2pR.tensor, c2off, [[1023, 128], [1, 128]])
                            p.do("sync", (lambda c2view=c2view: (lambda e: e.dma_start_transpose(
                                out=cg_bf[:], in_=c2view)))(),
                                waits=[("sta", p.cur("sta")), ("gF", p.cur("gF"))], inc="gdma", inck=16)
                            pcoff = (k0 + 2) * 1024 + q0 - k0 + 512
                            pcview = bass.AP(p2cA.tensor, pcoff, [[1023, 128], [1, 128]])
                            p.dma("sync", pg_bf[:], pcview,
                                  waits=[("sta", p.cur("sta")), ("gF", p.cur("gF"))], inc="gdma")
                        if 128 <= J00 <= 896:
                            p.do("vector", lambda e: e.tensor_add(g_t[:], cg_bf[:], pg_bf[:]),
                                 waits=["gdma"], inc="gF")
                            p.do("vector", (lambda kt=kt: (lambda e: e.tensor_add(
                                AR5[:, kt * 128:(kt + 1) * 128], ps_s[:, 0:128], g_t[:])))(),
                                waits=["mms", "gF"], inc="sevt")
                        elif J00 >= 1152 or J00 <= -128:
                            sel = 1 if J00 >= 1152 else 0
                            cct = cc_hi if sel else cc_lo
                            p.do("vector", (lambda kt=kt, cct=cct, sel=sel: (lambda e: e.scalar_tensor_tensor(
                                AR5[:, kt * 128:(kt + 1) * 128], cct[:],
                                pconsts[:, kt * 2 + sel: kt * 2 + sel + 1], ps_s[:, 0:128],
                                ALU.add, ALU.add)))(), waits=["mms", "cF"], inc="sevt")
                        else:
                            lo = (J00 == 0)
                            m_in = mge if lo else mlt
                            m_out = mlt if lo else mge
                            sel = 0 if lo else 1
                            cct = cc_lo if lo else cc_hi
                            p.do("vector", lambda e: e.tensor_add(g_t[:], cg_bf[:], pg_bf[:]),
                                 waits=["gdma"], inc="gF")
                            p.do("vector", (lambda m_in=m_in: (lambda e: e.tensor_mul(
                                g2_t[:], g_t[:], m_in[:])))(), waits=["gF"], inc="gF")
                            p.do("vector", (lambda kt=kt, cct=cct, sel=sel: (lambda e: e.tensor_scalar_add(
                                t1_t[:], cct[:], pconsts[:, kt * 2 + sel: kt * 2 + sel + 1])))(),
                                waits=["cF", ("gF", p.cur("gF"))], inc="gF")
                            p.do("vector", (lambda m_out=m_out: (lambda e: e.tensor_mul(
                                t1_t[:], t1_t[:], m_out[:])))(), waits=["gF"], inc="gF")
                            p.do("vector", lambda e: e.tensor_add(g2_t[:], g2_t[:], t1_t[:]),
                                 waits=["gF"], inc="gF")
                            p.do("vector", (lambda kt=kt: (lambda e: e.tensor_add(
                                AR5[:, kt * 128:(kt + 1) * 128], ps_s[:, 0:128], g2_t[:])))(),
                                waits=["mms", "gF"], inc="sevt")
                        p.do("vector", (lambda kt=kt: (lambda e: e.reduce_max(
                            maxcols[:, kt:kt + 1], AR5[:, kt * 128:(kt + 1) * 128], axis=AX.X)))(),
                            waits=["sevt"], inc="sevt")
                    p.do("vector", lambda e: e.reduce_max(colmax[:], maxcols[:], axis=AX.X),
                         waits=["sevt"], inc="cF")
                    p.do("tensor", lambda e: nc.tensor.transpose(
                        ps_r[0:1, 0:128], colmax[:], identf[:]), waits=["cF"], inc="mma")
                    p.do("vector", lambda e: e.reduce_max(m11[:], ps_r[0:1, 0:128], axis=AX.X),
                         waits=["mma"], inc="cF")
                    p.do("tensor", lambda e: nc.tensor.matmul(
                        ps_bc[:, 0:1], negones_f[:], m11[:], start=True, stop=True),
                        waits=["cF"], inc="mma")
                    p.do("vector", lambda e: e.tensor_copy(negm_sb[:], ps_bc[:, 0:1]),
                         waits=["mma"], inc="cF")
                    for kt in range(16):
                        p.do("scalar", (lambda kt=kt: (lambda e: e.activation(
                            ebf[:, kt * 128:(kt + 1) * 128], AR5[:, kt * 128:(kt + 1) * 128],
                            AF.Exp, bias=negm_sb[:])))(), waits=["cF"], inc="eexp")
                    for kt in range(16):
                        p.do("tensor", (lambda kt=kt: (lambda e: nc.tensor.matmul(
                            ps_r[0:1, 0:128], onesc_bf[:], ebf[:, kt * 128:(kt + 1) * 128],
                            start=(kt == 0), stop=(kt == 15))))(),
                            waits=(["eexp"] if kt == 0 else []), inc=("mmr" if kt == 15 else None))
                    for kt in range(16):
                        p.do("tensor", (lambda kt=kt, h=h: (lambda e: nc.tensor.matmul(
                            ps_c[0:64, 0:128], AR4[:, kt * 1024 + h * 64: kt * 1024 + h * 64 + 64],
                            ebf[:, kt * 128:(kt + 1) * 128],
                            start=(kt == 0), stop=(kt == 15))))(),
                            waits=(["eexp", ("cevt", p.cur("cevt"))] if kt == 0 else []),
                            inc=("mmc" if kt == 15 else None))
                    p.do("vector", lambda e: e.reciprocal(recr[:], ps_r[0:1, 0:128]),
                         waits=["mmr"], inc="cF")
                    p.do("tensor", lambda e: nc.tensor.matmul(
                        ps_bc[:, 128:256], ones_f[:], recr[:], start=True, stop=True),
                        waits=["cF"], inc="mmn")
                    p.do("vector", lambda e: e.tensor_copy(ctxs[:], ps_c[0:64, 0:128]),
                         waits=["mmc"], inc="cevt")
                    p.do("vector", (lambda hb=hb, hc=hc, q0=q0: (lambda e: e.tensor_mul(
                        AR6[hb:hb + 64, hc * 2048 + q0: hc * 2048 + q0 + 128],
                        ctxs[:], ps_bc[0:64, 128:256])))(), waits=["mmn", "cevt"], inc="cF")

            for tt in range(16):
                p.dma("sync", xtile[:], x_d[brow + tt * 128: brow + (tt + 1) * 128, :],
                      waits=[("c", p.cur("c")), ("cF", p.cur("cF")), ("stF", p.cur("stF"))], inc="ld")
                for mc in range(8):
                    p.do("tensor", (lambda mc=mc, tt=tt: (lambda e: nc.tensor.transpose(
                        ps_t[:], AR6[:, mc * 2048 + tt * 128: mc * 2048 + tt * 128 + 128],
                        ident[:])))(), waits=[("cF", p.cur("cF")), ("oevt", p.cur("oevt")), "ld"], inc="mmo")
                    p.do("vector", (lambda mc=mc: (lambda e: e.tensor_add(
                        xtile[:, mc * 128:(mc + 1) * 128], ps_t[:],
                        xtile[:, mc * 128:(mc + 1) * 128])))(), waits=["mmo"], inc="oevt")
                p.dma("gpsimd", y_d[brow + tt * 128: brow + (tt + 1) * 128, :], xtile[:],
                      waits=[("oevt", p.cur("oevt"))], inc="stF")
        p.do("gpsimd", lambda e: e.engine_nop(), waits=["stF"], inc=None)
        p.emit(nc, ctx)
    return nc


def prep_attn_weights(d):
    import ml_dtypes
    out = {}
    out["rel_dr"] = np.ascontiguousarray(d["L0_attn_rel_embeddings"])
    out["wq_dr"] = np.ascontiguousarray(d["L0_attn_Wq"])
    out["wk_dr"] = np.ascontiguousarray(d["L0_attn_Wk"])
    out["wv_dr"] = np.ascontiguousarray(d["L0_attn_Wv"])
    out["wpk_dr"] = np.ascontiguousarray(d["L0_attn_Wpk"])
    out["wpq_dr"] = np.ascontiguousarray(d["L0_attn_Wpq"])
    out["bqkv_dr"] = np.ascontiguousarray(np.concatenate(
        [d["L0_attn_bq"].reshape(8, 128).T, d["L0_attn_bk"].reshape(8, 128).T,
         d["L0_attn_bv"].reshape(8, 128).T], axis=1))
    out["bvr_dr"] = np.ascontiguousarray(d["L0_attn_bv"].reshape(1, 1024))
    out["bpq_dr"] = np.ascontiguousarray(d["L0_attn_bpq"].reshape(8, 128).T)
    out["idmat"] = np.eye(128).astype(ml_dtypes.bfloat16)
    out["idmatf"] = np.eye(128).astype(np.float32)
    ii = np.arange(128)[:, None]; jj = np.arange(128)[None, :]
    out["mge_dr"] = (jj >= ii).astype(np.float32)
    out["mlt_dr"] = (jj < ii).astype(np.float32)
    return out



_PROGS = {}

def _get_progs():
    if not _PROGS:
        _PROGS["A"] = build_blockA()
        _PROGS["B"] = build_blockB()
        _PROGS["C"] = build_blockC()
        _PROGS["D"] = build_blockD()
    return _PROGS


def kernel(**inputs):
    import ml_dtypes
    d = inputs
    progs = _get_progs()
    x = np.ascontiguousarray(d["L0_attn_hidden_states"].reshape(16, 2048, 1024))
    shards = [np.ascontiguousarray(x[2 * i:2 * i + 2].reshape(T_ROWS, D)) for i in range(NCORES)]

    wA = prep_attn_weights(d)
    resA = run_bass_kernel_spmd(progs["A"], [dict(x=s, **wA) for s in shards],
                                core_ids=list(range(NCORES)))
    shards = [np.asarray(r["y"]) for r in resA.results]

    wB = prep_ffn_weights(d)
    wB["idmatf"] = np.eye(128).astype(np.float32)
    resB = run_bass_kernel_spmd(progs["B"], [dict(x=s, **wB) for s in shards],
                                core_ids=list(range(NCORES)))
    shards = [np.asarray(r["y"]) for r in resB.results]

    wC = prep_mamba_weights(d)
    resC = run_bass_kernel_spmd(progs["C"], [dict(x=s, **wC) for s in shards],
                                core_ids=list(range(NCORES)))
    shards = [np.asarray(r["y"]) for r in resC.results]

    wD = {
        "w": np.ascontiguousarray(d["L1_ffn_weight"]),
        "b": np.ascontiguousarray(d["L1_ffn_bias"]).reshape(1, 1024),
        "idmat": np.eye(128).astype(ml_dtypes.bfloat16),
    }
    resD = run_bass_kernel_spmd(progs["D"], [dict(x=s, **wD) for s in shards],
                                core_ids=list(range(NCORES)))
    shards = [np.asarray(r["y"]) for r in resD.results]

    out = np.concatenate([s.reshape(2, 2048, 1024) for s in shards], axis=0)
    return out.astype(np.float32)


# version 02 (final)
# speedup vs baseline: 1.0013x; 1.0013x over previous
import numpy as np
from contextlib import ExitStack
import concourse.bass as bass
import concourse.mybir as mybir
from concourse.bass_utils import run_bass_kernel_spmd

F32 = mybir.dt.float32
BF16 = mybir.dt.bfloat16
FP8 = mybir.dt.float8e4
U8 = mybir.dt.uint8
AF = mybir.ActivationFunctionType
ALU = mybir.AluOpType
AX = mybir.AxisListType

NCORES = 8
B_CORE = 2
S = 2048
D = 1024
T_ROWS = B_CORE * S
NTILES = T_ROWS // 128


ENGINE_ORDER = ("sync", "vector", "scalar", "tensor", "gpsimd")

class Plan:
    def __init__(self):
        self.steps = []
        self.semcnt = {}

    def cur(self, sem):
        return self.semcnt.get(sem, 0)

    def do(self, eng, fn, waits=None, inc=None, inck=1):
        ws = []
        for w in (waits or []):
            if isinstance(w, tuple):
                if w[1] > 0:
                    ws.append((w[0], w[1]))
            else:
                v = self.cur(w)
                if v > 0:
                    ws.append((w, v))
        incs = []
        if inc is not None:
            self.semcnt[inc] = self.cur(inc) + inck
            incs.append((inc, inck))
        self.steps.append((eng, fn, ws, incs))
        return self.cur(inc) if inc is not None else None

    def barrier(self):
        allw = [(s, c) for s, c in self.semcnt.items() if c > 0 and s != "bar"]
        self.do("gpsimd", lambda e: e.engine_nop(), waits=allw, inc="bar")
        b = self.cur("bar")
        for eng in ("sync", "vector", "scalar", "tensor"):
            self.do(eng, None, waits=[("bar", b)])

    def dma(self, eng, out, in_, waits=None, inc=None):
        return self.do(eng, mk_dma(out, in_), waits=waits, inc=inc, inck=16)

    def emit(self, nc, ctx):
        sems = {name: ctx.enter_context(nc.semaphore(name)) for name in self.semcnt}
        block = ctx.enter_context(nc.Block())
        for engname in ENGINE_ORDER:
            mysteps = [s for s in self.steps if s[0] == engname]
            if not mysteps:
                continue
            def body(eng, mysteps=mysteps):
                for _, fn, ws, incs in mysteps:
                    for (s, v) in ws:
                        eng.wait_ge(sems[s], v)
                    if fn is None:
                        continue
                    instr = fn(eng)
                    for (s, k) in incs:
                        instr.then_inc(sems[s], k)
            getattr(block, engname)(body)


def mk_dma(out, in_):
    return lambda e: e.dma_start(out=out, in_=in_)



def emit_layernorm(p, sb, tile_f32, xln_out, tmp, cols, sem, eng="vector"):
    m, v, r = cols
    n = tile_f32.shape[1] if hasattr(tile_f32, 'shape') else 1024
    p.do(eng, lambda e, a=tile_f32, o=m: e.reduce_sum(o[:], a[:], axis=AX.X), waits=[sem], inc=sem)
    p.do(eng, lambda e, o=m: e.tensor_scalar_mul(o[:], o[:], 1.0 / 1024), waits=[sem], inc=sem)
    p.do(eng, lambda e, a=tile_f32, o=tmp: e.tensor_scalar_sub(o[:], a[:], m[:]), waits=[sem], inc=sem)
    p.do("scalar", lambda e: e.activation(xln_out[:], tmp[:], AF.Square, accum_out=v[:]), waits=[sem], inc=sem)
    p.do(eng, lambda e: e.tensor_scalar(v[:], v[:], 1.0 / 1024, 1e-5, ALU.mult, ALU.add), waits=[sem], inc=sem)
    p.do("scalar", lambda e: e.activation(v[:], v[:], AF.Sqrt), waits=[sem], inc=sem)
    p.do(eng, lambda e: e.reciprocal(r[:], v[:]), waits=[sem], inc=sem)
    p.do(eng, lambda e: e.tensor_scalar_mul(xln_out[:], tmp[:], r[:]), waits=[sem], inc=sem)


def emit_quant_fp6(p, q, names, sem):
    src, dst = q['src'], q['dst']
    amax, sc, rsc = q['amax'], q['sc'], q['rsc']
    y, ay, mhi, q1, f8, w32, d1 = q['y'], q['ay'], q['mhi'], q['q1'], q['f8'], q['w32'], q['d1']
    n = 1024
    nb = n // 32
    def v3(t, steps):
        return bass.AP(t[:].tensor, 0, steps)
    src3 = v3(src, [[n, 128], [32, nb], [1, 32]])
    y3 = v3(y, [[n, 128], [32, nb], [1, 32]])
    sc3 = v3(sc, [[nb, 128], [1, nb], [0, 32]])
    rsc3 = v3(rsc, [[nb, 128], [1, nb], [0, 32]])
    q13 = v3(q1, [[n, 128], [32, nb], [1, 32]])
    dst3 = v3(dst, [[n, 128], [32, nb], [1, 32]])

    p.do("vector", lambda e: e.tensor_reduce(amax[:], src3, AX.X, ALU.max, apply_absolute_value=True), waits=[sem], inc=sem)
    p.do("vector", lambda e: e.tensor_scalar_mul(sc[:], amax[:], float(np.float32(1.0) / np.float32(7.5))), waits=[sem], inc=sem)
    p.do("vector", lambda e: e.tensor_single_scalar(sc[:], sc[:], 1e-30, ALU.max), waits=[sem], inc=sem)
    p.do("vector", lambda e: e.reciprocal(rsc[:], sc[:]), waits=[sem], inc=sem)
    p.do("vector", lambda e: e.tensor_tensor(y3, src3, rsc3, ALU.mult), waits=[sem], inc=sem)
    p.do("scalar", lambda e: e.activation(ay[:], y[:], AF.Abs), waits=[sem], inc=sem)
    p.do("vector", lambda e: e.tensor_single_scalar(mhi[:], ay[:], 1.9375, ALU.is_ge), waits=[sem], inc=sem)
    p.do("vector", lambda e: e.tensor_scalar_add(q1[:], y[:], 1572864.0), waits=[sem], inc=sem)
    p.do("vector", lambda e: e.tensor_scalar_add(q1[:], q1[:], -1572864.0), waits=[sem], inc=sem)
    p.do("vector", lambda e: e.tensor_scalar_mul(f8[:], y[:], 32.0), waits=[sem], inc=sem)
    p.do("vector", lambda e: e.tensor_scalar_mul(w32[:], f8[:], 1.0 / 32), waits=[sem], inc=sem)
    p.do("vector", lambda e: e.tensor_sub(d1[:], w32[:], q1[:]), waits=[sem], inc=sem)
    p.do("vector", lambda e: e.tensor_mul(d1[:], d1[:], mhi[:]), waits=[sem], inc=sem)
    p.do("vector", lambda e: e.tensor_add(q1[:], q1[:], d1[:]), waits=[sem], inc=sem)
    p.do("vector", lambda e: e.tensor_tensor(dst3, q13, sc3, ALU.mult), waits=[sem], inc=sem)



def emit_blockD(nc, p, DD, x_d, y_d):
    w_d = DD["wD_dr"]
    b_d = DD["bD_dr"]
    id_d = DD["idmat"]

    ctx = ExitStack()
    with ctx:
        sb = lambda name, shp, dt=F32: ctx.enter_context(nc.sbuf_tensor("D" + name, shp, dt))
        ps = lambda name, shp, dt=F32: ctx.enter_context(nc.psum_tensor("D" + name, shp, dt))

        ident = sb("ident", [128, 128], BF16)
        ones_bf = sb("ones_bf", [1, 128], BF16)
        bias_bf = sb("bias_bf", [1, D], BF16)
        wdT = sb("wdT", [128, 8 * D], BF16)
        wtile = sb("wtile", [128, D])
        wd = sb("wd", [128, D], BF16)
        q = dict(src=wtile, dst=wd,
                 amax=sb("amax", [128, 32]), sc=sb("sc", [128, 32]), rsc=sb("rsc", [128, 32]),
                 y=sb("qy", [128, D]), ay=sb("qay", [128, D]), mhi=sb("qmhi", [128, D]),
                 q1=sb("qq1", [128, D]), f8=sb("qf8", [128, D], FP8), w32=sb("qw32", [128, D]),
                 d1=sb("qd1", [128, D]))
        ps_t = ps("ps_t", [128, 128], BF16)
        xtile = sb("xtile", [128, D])
        xln = sb("xln", [128, D])
        tmp = sb("tmp", [128, D])
        xd = sb("xd", [128, D], BF16)
        xdT = sb("xdT", [128, D], BF16)
        cols = (sb("c_m", [128, 1]), sb("c_v", [128, 1]), sb("c_r", [128, 1]))
        ps_o = [ps("ps_o0", [128, 512]), ps("ps_o1", [128, 512])]
        out_sb = sb("out_sb", [128, D])

        p.dma("sync", ident[:], id_d, inc="ld")
        p.dma("gpsimd", bias_bf[:], b_d, inc="ldg")
        p.do("vector", lambda e: e.memset(ones_bf[:], 1.0), inc="c")

        for wi in range(8):
            p.dma("sync", wtile[:], w_d[wi * 128:(wi + 1) * 128, :], waits=["wfree"], inc="ld")
            p.do("vector", lambda e: e.engine_nop(), waits=["ld"], inc="c")
            emit_quant_fp6(p, q, f"w{wi}", "c")
            for wj in range(8):
                p.do("tensor", (lambda wj=wj: (lambda e: nc.tensor.transpose(
                    ps_t[:], wd[:, wj * 128:(wj + 1) * 128], ident[:])))(),
                    waits=["c", "ld", "evt"], inc="tt")
                p.do("vector", (lambda wi=wi, wj=wj: (lambda e: e.tensor_copy(
                    wdT[:, wj * 1024 + wi * 128: wj * 1024 + (wi + 1) * 128], ps_t[:])))(),
                    waits=["tt"], inc="evt")
            p.do("vector", lambda e: e.engine_nop(), waits=["c", "evt"], inc="wfree", inck=16)

        for tt in range(NTILES):
            xs = x_d[tt * 128:(tt + 1) * 128, :]
            p.dma("sync", xtile[:], xs, waits=[("xfree", p.cur("xfree"))], inc="ld")
            p.do("vector", lambda e: e.engine_nop(), waits=["ld"], inc="c")
            emit_layernorm(p, sb, xtile, xln, tmp, cols, "c")
            q2 = dict(q); q2['src'] = xln; q2['dst'] = xd
            emit_quant_fp6(p, q2, f"x{tt}", "c")
            for wj in range(8):
                p.do("tensor", (lambda wj=wj: (lambda e: nc.tensor.transpose(
                    ps_t[:], xd[:, wj * 128:(wj + 1) * 128], ident[:])))(),
                    waits=["c", "evt"], inc="tt")
                p.do("vector", (lambda wj=wj: (lambda e: e.tensor_copy(
                    xdT[:, wj * 128:(wj + 1) * 128], ps_t[:])))(),
                    waits=["tt"], inc="evt")
            for mh in range(2):
                for kc in range(8):
                    p.do("tensor", (lambda mh=mh, kc=kc: (lambda e: nc.tensor.matmul(
                        ps_o[mh][:], xdT[:, kc * 128:(kc + 1) * 128],
                        wdT[:, kc * 1024 + mh * 512: kc * 1024 + mh * 512 + 512],
                        start=(kc == 0), stop=False)))(),
                        waits=(["evt", "mmdone"] if kc == 0 else []), inc=None)
                p.do("tensor", (lambda mh=mh: (lambda e: nc.tensor.matmul(
                    ps_o[mh][:], ones_bf[:], bias_bf[:, mh * 512:mh * 512 + 512],
                    start=False, stop=True)))(),
                    waits=(["ldg"] if tt == 0 and mh == 0 else []), inc="mm")
                p.do("vector", (lambda mh=mh: (lambda e: e.tensor_add(
                    out_sb[:, mh * 512:mh * 512 + 512], ps_o[mh][:],
                    xtile[:, mh * 512:mh * 512 + 512])))(),
                    waits=["mm", ("st", p.cur("st"))], inc="mmdone")
            p.do("vector", lambda e: e.engine_nop(), waits=["mmdone"], inc="xfree", inck=16)
            p.dma("gpsimd", y_d[tt * 128:(tt + 1) * 128, :], out_sb[:],
                  waits=[("mmdone", p.cur("mmdone"))], inc="st")
        p.do("gpsimd", lambda e: e.engine_nop(), waits=["st"], inc=None)



TC = 256
NTC = S // TC

def emit_blockC(nc, p, DD, x_d, y_d):
    ipT_d = DD["ipT_dr"]
    opT_d = DD["opT_dr"]
    xpT_d = DD["xpT_dr"]
    dpT_d = DD["dpT_dr"]
    convw_d = DD["convw_dr"]
    convb_d = DD["convb_dr"]
    dtb_d = DD["dtb_dr"]
    alog_d = DD["alog_dr"]
    dvec_d = DD["dvec_dr"]
    nw_d = DD["nw_dr"]
    id_d = DD["idmat"]
    sel_d = DD["selmat"]

    ctx = ExitStack()
    with ctx:
        sb = lambda name, shp, dt=F32: ctx.enter_context(nc.sbuf_tensor("C" + name, shp, dt))
        ps = lambda name, shp, dt=F32: ctx.enter_context(nc.psum_tensor("C" + name, shp, dt))

        ident = sb("ident", [128, 128], BF16)
        selm = sb("selm", [32, 4096], BF16)
        convw = sb("convw", [128, 64]); convb = sb("convb", [128, 16])
        dtb = sb("dtb", [128, 16]); alog = sb("alog", [128, 256])
        dvec = sb("dvec", [128, 16]); nw = sb("nw", [128, 8])
        aneg = sb("aneg", [128, 256])
        opT = sb("opT", [128, 16 * 1024], BF16)
        xpT = sb("xpT", [128, 16 * 96], BF16)
        dpT = sb("dpT", [64, 2048], BF16)
        ipT = sb("ipT", [128, 8 * 2048], BF16)
        hfm = sb("hfm", [128, 8 * TC], BF16)
        y3bf = sb("y3bf", [128, 16 * TC], BF16)
        xpart = sb("xpart", [128, 16 * (TC + 3)])
        xc = sb("xc", [128, 16 * TC])
        zsl = sb("zsl", [128, 16 * TC])
        dts = sb("dts", [128, 16 * TC])
        ysl = sb("ysl", [128, 16 * TC])
        xdbl = sb("xdbl", [96, TC])
        bcs = sb("bcs", [32, TC], BF16)
        states = sb("states", [128, 256])
        x_tm = [sb(f"x_tm{i}", [128, D]) for i in range(TC // 128)]
        xln = sb("xln", [128, D]); tmp = sb("tmp", [128, D])
        h_bf = sb("h_bf", [128, D], BF16)
        cols = (sb("c_m", [128, 1]), sb("c_v", [128, 1]), sb("c_r", [128, 1]))
        s2c = sb("s2c", [128, 1]); rmc = sb("rmc", [128, 1])
        xc_bf = sb("xc_bf", [128, TC], BF16)
        dtpre = sb("dtpre", [64, TC], BF16)
        pre_t = sb("pre_t", [128, TC]); at_t = sb("at_t", [128, TC])
        et_t = sb("et_t", [128, TC])
        Bb = sb("Bb", [128, TC]); Cb = sb("Cb", [128, TC])
        dA_t = sb("dA_t", [128, TC]); dBu_t = sb("dBu_t", [128, TC])
        h_t = sb("h_t", [128, TC]); tp_t = sb("tp_t", [128, TC])

        ps_t = ps("ps_t", [128, 128], BF16)
        ps_a = ps("ps_a", [128, TC])
        ps_96 = ps("ps_96", [96, TC])
        ps_bc = ps("ps_bc", [128, TC])
        ps_o = [ps("ps_o0", [128, 512]), ps("ps_o1", [128, 512])]

        TCP = TC + 3

        for dst, src in [(ident, id_d), (selm, sel_d)]:
            p.dma("sync", dst[:], src, inc="ld")
        for dst, src in [(convw, convw_d), (convb, convb_d), (dtb, dtb_d),
                         (alog, alog_d), (dvec, dvec_d), (nw, nw_d)]:
            p.dma("sync", dst[:], src, inc="ld")
        for kc in range(16):
            p.dma("gpsimd", opT[:, kc * 1024:(kc + 1) * 1024],
                  opT_d[kc * 128:(kc + 1) * 128, :], inc="ld")
            p.dma("gpsimd", xpT[:, kc * 96:(kc + 1) * 96],
                  xpT_d[kc * 128:(kc + 1) * 128, :], inc="ld")
        p.dma("gpsimd", dpT[:], dpT_d, inc="ld")
        p.do("scalar", lambda e: e.activation(aneg[:], alog[:], AF.Exp), waits=["ld"], inc="c")
        p.do("vector", lambda e: e.tensor_scalar_mul(aneg[:], aneg[:], -1.0), waits=["c", "ld"], inc="c")

        for b in range(B_CORE):
            p.do("vector", lambda e: e.memset(states[:], 0.0), waits=["c"], inc="c")
            p.do("vector", lambda e: e.memset(xpart[:], 0.0), waits=["c"], inc="c")
            for tc in range(NTC):
                row0 = b * S + tc * TC
                if tc > 0:
                    for dti in range(16):
                        p.do("vector", (lambda dti=dti: (lambda e: e.tensor_copy(
                            xpart[:, dti * TCP:dti * TCP + 3],
                            xpart[:, dti * TCP + TC:dti * TCP + TC + 3])))(),
                            waits=["c"], inc="c")
                for ts in range(TC // 128):
                    p.dma("sync", x_tm[ts][:], x_d[row0 + ts * 128: row0 + (ts + 1) * 128, :],
                          waits=[("c", p.cur("c")), ("mmoF", p.cur("mmoF")), ("st", p.cur("st"))], inc="ld")
                    p.do("vector", lambda e: e.engine_nop(), waits=["ld"], inc="c")
                    emit_layernorm(p, sb, x_tm[ts], xln, tmp, cols, "c")
                    p.do("scalar", lambda e: e.activation(tmp[:], xln[:], AF.Square, accum_out=s2c[:]), waits=["c"], inc="c")
                    p.do("scalar", lambda e: e.activation(s2c[:], s2c[:], AF.Sqrt), waits=["c"], inc="c")
                    p.do("vector", lambda e: e.tensor_scalar(s2c[:], s2c[:], 1.0 / 32, 1e-5, ALU.mult, ALU.add), waits=["c"], inc="c")
                    p.do("vector", lambda e: e.reciprocal(rmc[:], s2c[:]), waits=["c"], inc="c")
                    p.do("vector", lambda e: e.tensor_scalar_mul(h_bf[:], xln[:], rmc[:]), waits=["c"], inc="c")
                    for kc in range(8):
                        p.do("tensor", (lambda kc=kc: (lambda e: nc.tensor.transpose(
                            ps_t[:], h_bf[:, kc * 128:(kc + 1) * 128], ident[:])))(),
                            waits=["c", "evt"], inc="tt")
                        p.do("vector", (lambda kc=kc, ts=ts: (lambda e: e.tensor_scalar_mul(
                            hfm[:, kc * TC + ts * 128: kc * TC + ts * 128 + 128],
                            ps_t[:], nw[:, kc:kc + 1])))(),
                            waits=["tt"], inc="evt")
                for half in range(2):
                    for dc in range(8):
                        p.dma("gpsimd", ipT[:, dc * 2048:(dc + 1) * 2048],
                              ipT_d[dc * 128:(dc + 1) * 128, half * 2048:(half + 1) * 2048],
                              waits=[("mmF", p.cur("mmF"))], inc="ldw")
                    for oc in range(16):
                        jc = half * 16 + oc
                        for dc in range(8):
                            p.do("tensor", (lambda dc=dc, oc=oc: (lambda e: nc.tensor.matmul(
                                ps_a[:], ipT[:, dc * 2048 + oc * 128: dc * 2048 + oc * 128 + 128],
                                hfm[:, dc * TC:(dc + 1) * TC],
                                start=(dc == 0), stop=(dc == 7))))(),
                                waits=(["ldw", "evt", "mmF"] if dc == 0 else []),
                                inc=("mm" if dc == 7 else None))
                        if jc < 16:
                            p.do("vector", (lambda jc=jc: (lambda e: e.tensor_copy(
                                xpart[:, jc * TCP + 3: jc * TCP + 3 + TC], ps_a[:])))(),
                                waits=["mm"], inc="mmF")
                        else:
                            p.do("scalar", (lambda jc=jc: (lambda e: e.activation(
                                zsl[:, (jc - 16) * TC:(jc - 16) * TC + TC], ps_a[:], AF.Silu)))(),
                                waits=["mm"], inc="mmF")
                for dti in range(16):
                    p.do("vector", (lambda dti=dti: (lambda e: e.tensor_scalar(
                        dA_t[:], xpart[:, dti * TCP: dti * TCP + TC],
                        convw[:, dti * 4: dti * 4 + 1], convb[:, dti:dti + 1],
                        ALU.mult, ALU.add)))(), waits=["c", "mmF"], inc="c")
                    for k in range(1, 4):
                        p.do("vector", (lambda dti=dti, k=k: (lambda e: e.scalar_tensor_tensor(
                            dA_t[:], xpart[:, dti * TCP + k: dti * TCP + k + TC],
                            convw[:, dti * 4 + k: dti * 4 + k + 1], dA_t[:],
                            ALU.mult, ALU.add)))(), waits=["c"], inc="c")
                    p.do("scalar", (lambda dti=dti: (lambda e: e.activation(
                        xc[:, dti * TC:(dti + 1) * TC], dA_t[:], AF.Silu)))(),
                        waits=["c"], inc="c")
                for dti in range(16):
                    p.do("vector", (lambda dti=dti: (lambda e: e.tensor_copy(
                        xc_bf[:], xc[:, dti * TC:(dti + 1) * TC])))(),
                        waits=["c", ("mm96", p.cur("mm96"))], inc="c")
                    p.do("tensor", (lambda dti=dti: (lambda e: nc.tensor.matmul(
                        ps_96[:], xpT[:, dti * 96:(dti + 1) * 96], xc_bf[:],
                        start=(dti == 0), stop=(dti == 15))))(),
                        waits=["c"], inc="mm96")
                p.do("vector", lambda e: e.tensor_copy(xdbl[:], ps_96[:]), waits=[("mm96", p.cur("mm96"))], inc="c")
                p.do("vector", lambda e: e.tensor_copy(dtpre[:], xdbl[0:64, :]), waits=["c"], inc="c")
                p.do("vector", lambda e: e.tensor_copy(bcs[:], xdbl[64:96, :]), waits=["c"], inc="c")
                for dc in range(16):
                    p.do("tensor", (lambda dc=dc: (lambda e: nc.tensor.matmul(
                        ps_a[:], dpT[:, dc * 128:(dc + 1) * 128], dtpre[:],
                        start=True, stop=True)))(), waits=["c", "dtF"], inc="mmdt")
                    p.do("vector", (lambda dc=dc: (lambda e: e.tensor_scalar_add(
                        pre_t[:], ps_a[:], dtb[:, dc:dc + 1])))(), waits=["mmdt"], inc="dtF")
                    p.do("scalar", lambda e: e.activation(at_t[:], pre_t[:], AF.Abs), waits=["dtF"], inc="c")
                    p.do("scalar", lambda e: e.activation(et_t[:], at_t[:], AF.Exp, scale=-1.0), waits=["c"], inc="c")
                    p.do("scalar", lambda e: e.activation(at_t[:], et_t[:], AF.Ln, bias=1.0), waits=["c"], inc="c")
                    p.do("vector", lambda e: e.tensor_relu(et_t[:], pre_t[:]), waits=["c"], inc="c")
                    p.do("vector", (lambda dc=dc: (lambda e: e.tensor_add(
                        dts[:, dc * TC:(dc + 1) * TC], at_t[:], et_t[:])))(),
                        waits=["c"], inc="c")
                for dti in range(16):
                    p.do("vector", (lambda dti=dti: (lambda e: e.tensor_mul(
                        xpart[:, dti * TCP:dti * TCP + TC], dts[:, dti * TC:(dti + 1) * TC],
                        xc[:, dti * TC:(dti + 1) * TC])))(), waits=["c"], inc="c")
                for n in range(16):
                    p.do("tensor", (lambda n=n: (lambda e: nc.tensor.matmul(
                        ps_bc[:], selm[:, n * 128:(n + 1) * 128], bcs[:],
                        start=True, stop=True)))(), waits=["c", "bcF"], inc="mmbc")
                    p.do("vector", lambda e: e.tensor_copy(Bb[:], ps_bc[:]), waits=["mmbc"], inc="bcF")
                    p.do("tensor", (lambda n=n: (lambda e: nc.tensor.matmul(
                        ps_bc[:], selm[:, (16 + n) * 128:(17 + n) * 128], bcs[:],
                        start=True, stop=True)))(), waits=["bcF"], inc="mmbc")
                    p.do("vector", lambda e: e.tensor_copy(Cb[:], ps_bc[:]), waits=["mmbc"], inc="bcF")
                    for dti in range(16):
                        idx = dti * 16 + n
                        p.do("scalar", (lambda dti=dti, n=n: (lambda e: e.activation(
                            dA_t[:], dts[:, dti * TC:(dti + 1) * TC], AF.Exp,
                            scale=aneg[:, dti * 16 + n: dti * 16 + n + 1])))(),
                            waits=["c"], inc="c")
                        p.do("vector", (lambda dti=dti: (lambda e: e.tensor_mul(
                            dBu_t[:], xpart[:, dti * TCP:dti * TCP + TC], Bb[:])))(),
                            waits=["c", "bcF"], inc="c")
                        p.do("vector", (lambda idx=idx: (lambda e: e.tensor_tensor_scan(
                            h_t[:], dA_t[:], dBu_t[:], states[:, idx:idx + 1],
                            ALU.mult, ALU.add)))(), waits=["c"], inc="c")
                        p.do("vector", (lambda idx=idx: (lambda e: e.tensor_copy(
                            states[:, idx:idx + 1], h_t[:, TC - 1:TC])))(),
                            waits=["c"], inc="c")
                        if n == 0:
                            p.do("vector", (lambda dti=dti: (lambda e: e.tensor_mul(
                                ysl[:, dti * TC:(dti + 1) * TC], h_t[:], Cb[:])))(),
                                waits=["c"], inc="c")
                        else:
                            p.do("vector", lambda e: e.tensor_mul(tp_t[:], h_t[:], Cb[:]), waits=["c"], inc="c")
                            p.do("vector", (lambda dti=dti: (lambda e: e.tensor_add(
                                ysl[:, dti * TC:(dti + 1) * TC],
                                ysl[:, dti * TC:(dti + 1) * TC], tp_t[:])))(),
                                waits=["c"], inc="c")
                for dti in range(16):
                    p.do("vector", (lambda dti=dti: (lambda e: e.scalar_tensor_tensor(
                        tp_t[:], xc[:, dti * TC:(dti + 1) * TC], dvec[:, dti:dti + 1],
                        ysl[:, dti * TC:(dti + 1) * TC], ALU.mult, ALU.add)))(),
                        waits=["c", "mmo"], inc="c")
                    p.do("vector", (lambda dti=dti: (lambda e: e.tensor_mul(
                        y3bf[:, dti * TC:(dti + 1) * TC], tp_t[:],
                        zsl[:, dti * TC:(dti + 1) * TC])))(), waits=["c"], inc="c")
                for ts in range(TC // 128):
                    for mh in range(2):
                        for kc in range(16):
                            p.do("tensor", (lambda ts=ts, mh=mh, kc=kc: (lambda e: nc.tensor.matmul(
                                ps_o[mh][:],
                                y3bf[:, kc * TC + ts * 128: kc * TC + ts * 128 + 128],
                                opT[:, kc * 1024 + mh * 512: kc * 1024 + mh * 512 + 512],
                                start=(kc == 0), stop=(kc == 15))))(),
                                waits=(["c", "mmoF"] if kc == 0 else []),
                                inc=("mmo" if kc == 15 else None))
                        p.do("vector", (lambda ts=ts, mh=mh: (lambda e: e.tensor_add(
                            x_tm[ts][:, mh * 512:(mh + 1) * 512], ps_o[mh][:],
                            x_tm[ts][:, mh * 512:(mh + 1) * 512])))(),
                            waits=["mmo", ("st", p.cur("st"))], inc="mmoF")
                    p.dma("gpsimd", y_d[row0 + ts * 128: row0 + (ts + 1) * 128, :], x_tm[ts][:],
                          waits=[("mmoF", p.cur("mmoF"))], inc="st")
        p.do("gpsimd", lambda e: e.engine_nop(), waits=["st"], inc=None)


def prep_mamba_weights(d):
    out = {}
    out["ipT_dr"] = np.ascontiguousarray(d["L1_attn_in_proj_w"].T)
    out["opT_dr"] = np.ascontiguousarray(d["L1_attn_out_proj_w"].T)
    out["xpT_dr"] = np.ascontiguousarray(d["L1_attn_x_proj_w"].T)
    out["dpT_dr"] = np.ascontiguousarray(d["L1_attn_dt_proj_w"].T)
    out["convw_dr"] = np.ascontiguousarray(
        d["L1_attn_conv_w"].reshape(16, 128, 4).transpose(1, 0, 2).reshape(128, 64))
    out["convb_dr"] = np.ascontiguousarray(d["L1_attn_conv_b"].reshape(16, 128).T)
    out["dtb_dr"] = np.ascontiguousarray(d["L1_attn_dt_proj_b"].reshape(16, 128).T)
    out["alog_dr"] = np.ascontiguousarray(
        d["L1_attn_A_log"].reshape(16, 128, 16).transpose(1, 0, 2).reshape(128, 256))
    out["dvec_dr"] = np.ascontiguousarray(d["L1_attn_D"].reshape(16, 128).T)
    out["nw_dr"] = np.ascontiguousarray(d["L1_attn_norm_weight"].reshape(8, 128).T)
    import ml_dtypes
    sel = np.zeros((32, 32, 128), np.float32)
    for n in range(32):
        sel[n, n, :] = 1.0
    out["selmat"] = np.ascontiguousarray(sel.transpose(1, 0, 2).reshape(32, 4096)).astype(ml_dtypes.bfloat16)
    out["idmat"] = np.eye(128).astype(ml_dtypes.bfloat16)
    return out





QW = 128
TB = 1024

def emit_blockB(nc, p, DD, x_d, y_d, dbg=False):
    qkvw_d = DD["qkvw_dr"]
    qkvbc_d = DD["qkvbc_dr"]
    vbr_d = DD["vbr_dr"]
    outw_d = DD["outw_dr"]
    outbr_d = DD["outbr_dr"]
    n1_d = DD["n1_dr"]
    n2_d = DD["n2_dr"]
    w1all_d = DD["w1all_dr"]
    w2all_d = DD["w2all_dr"]
    b1c_d = DD["b1c_dr"]
    b2c_d = DD["b2c_dr"]
    centT_d = DD["centT_dr"]
    rb_d = DD["rb_dr"]
    sel8_d = DD["sel8_dr"]
    id_d = DD["idmat"]
    idf_d = DD["idmatf"]
    cmask_d = DD["cmaskf_dr"]
    if dbg:
        dbg_attn = nc.dram_tensor("dbg_attn", [2 * 128, 16384], BF16, kind="ExternalOutput").ap()
        dbg_x1 = nc.dram_tensor("dbg_x1", [2 * 128, 16384], BF16, kind="ExternalOutput").ap()
        dbg_gat = nc.dram_tensor("dbg_gat", [2 * 8, 2048], BF16, kind="ExternalOutput").ap()
        dbg_q = nc.dram_tensor("dbg_q", [2 * 128, 16384], BF16, kind="ExternalOutput").ap()
        dbg_s = nc.dram_tensor("dbg_s", [128, 2048], F32, kind="ExternalOutput").ap()
        dbg_e = nc.dram_tensor("dbg_e", [128, 2048], BF16, kind="ExternalOutput").ap()
        dbg_nm = nc.dram_tensor("dbg_nm", [128, 1], F32, kind="ExternalOutput").ap()
        dbg_acc = nc.dram_tensor("dbg_acc", [128, 8192], F32, kind="ExternalOutput").ap()
        dbg_k = nc.dram_tensor("dbg_k", [2 * 128, 16384], BF16, kind="ExternalOutput").ap()
        dbg_v = nc.dram_tensor("dbg_v", [2 * 128, 16384], BF16, kind="ExternalOutput").ap()

    ctx = ExitStack()
    with ctx:
        sb = lambda name, shp, dt=F32: ctx.enter_context(nc.sbuf_tensor("B" + name, shp, dt))
        ps = lambda name, shp, dt=F32: ctx.enter_context(nc.psum_tensor("B" + name, shp, dt))

        AR1 = sb("AR1", [128, 16384], BF16)
        AR2 = sb("AR2", [128, 16384], BF16)
        AR3 = sb("AR3", [128, 16384], BF16)
        AR4 = sb("AR4", [128, 16384], BF16)
        AR5 = sb("AR5", [128, 2048])
        AR6 = sb("AR6", [128, 16384], BF16)
        ebf = sb("ebf", [128, 2048], BF16)
        ACC = AR4.bitcast(F32)

        ident = sb("ident", [128, 128], BF16)
        identf = sb("identf", [128, 128], F32)
        onescf = sb("onescf", [128, 1], F32)
        sel8 = sb("sel8", [8, 8 * 128], BF16)
        qkvbc = sb("qkvbc", [128, 24])
        qb8 = sb("qb8", [128, 8])
        vbr = sb("vbr", [1, 1024], BF16)
        outbr = sb("outbr", [1, 1024], BF16)
        n1c = sb("n1c", [128, 16]); n2c = sb("n2c", [128, 16])
        b1c = sb("b1c", [128, 9 * 16]); b2c = sb("b2c", [128, 9 * 8])
        centTf = sb("centTf", [128, 64], F32)
        xTf = sb("xTf", [128, 128], F32)
        logits_all = sb("logits_all", [128, 16 * 8], F32)
        rbrow = sb("rbrow", [1, 8])
        onesc_bf = sb("onesc_bf", [128, 1], BF16)
        onesr_bf = sb("onesr_bf", [1, 128], BF16)
        ones_f = sb("ones_f", [1, 128])
        negones_f = sb("negones_f", [1, 128])
        maxcols = sb("maxcols", [128, 16])
        colmax = sb("colmax", [128, 1])
        negm_sb = sb("negm_sb", [128, 1])
        cmask = sb("cmask", [128, 128])
        m11 = sb("m11", [1, 1])
        recr = sb("recr", [1, QW])
        ctxs = sb("ctxs", [64, QW])
        xtile = sb("xtile", [128, D])
        xln = sb("xln", [128, D]); tmp = sb("tmp", [128, D])
        xbf = sb("xbf", [128, D], BF16)
        cols = (sb("c_m", [128, 1]), sb("c_v", [128, 1]), sb("c_r", [128, 1]))
        logits = sb("logits", [128, 8])
        lmax = sb("lmax", [128, 1]); lsum = sb("lsum", [128, 1])
        sim = sb("sim", [128, 8]); vb = sb("vb", [128, 8]); vb2 = sb("vb2", [128, 8])
        mk1 = sb("mk1", [128, 8]); mk2 = sb("mk2", [128, 8])
        gsum = sb("gsum", [128, 1])
        gates_bf = sb("gates_bf", [128, 8], BF16)
        gatesT = sb("gatesT", [8, S], BF16)
        gb_sb = sb("gb_sb", [128, 512])
        t1_sb = sb("t1_sb", [128, 512])
        sqs = sb("sqs", [128, 256])
        statm = sb("statm", [128, 1]); statv = sb("statv", [128, 1])
        statm2 = sb("statm2", [128, 1])

        ps_s = ps("ps_s", [128, 512])
        ps_t = ps("ps_t", [128, 128], BF16)
        ps_r = ps("ps_r", [1, 256])
        ps_c = ps("ps_c", [128, 256])
        ps_bc = ps("ps_bc", [128, 256])
        ps_h = ps("ps_h", [128, 512])
        ps_o = ps("ps_o", [128, 512])
        ps_g = ps("ps_g", [128, 512])

        p.dma("sync", ident[:], id_d, inc="ld")
        p.dma("sync", identf[:], idf_d, inc="ld")
        p.dma("sync", cmask[:], cmask_d, inc="ld")
        p.dma("sync", sel8[:], sel8_d, inc="ld")
        p.dma("sync", qkvbc[:], qkvbc_d, inc="ld")
        p.dma("sync", n1c[:], n1_d, inc="ld")
        p.dma("sync", n2c[:], n2_d, inc="ld")
        p.dma("sync", b1c[:], b1c_d, inc="ld")
        p.dma("sync", b2c[:], b2c_d, inc="ld")
        p.dma("sync", rbrow[:], rb_d, inc="ld")
        p.dma("gpsimd", vbr[:], vbr_d, inc="ld")
        p.dma("gpsimd", outbr[:], outbr_d, inc="ld")
        for dc in range(8):
            p.dma("gpsimd", centTf[:, dc * 8:(dc + 1) * 8],
                  centT_d[dc * 128:(dc + 1) * 128, :], inc="ld")
        p.do("vector", lambda e: e.memset(onesc_bf[:], 1.0), waits=[], inc="c")
        p.do("vector", lambda e: e.memset(onesr_bf[:], 1.0), waits=[], inc="c")
        p.do("vector", lambda e: e.memset(onescf[:], 1.0), waits=[], inc="c")
        p.do("vector", lambda e: e.memset(ones_f[:], 1.0), waits=[], inc="c")
        p.do("vector", lambda e: e.memset(negones_f[:], -1.0), waits=[], inc="c")
        p.do("vector", lambda e: e.tensor_scalar_mul(qb8[:], qkvbc[:, 0:8], 0.125), waits=["ld"], inc="c")

        for b in range(B_CORE):
            brow = b * S

            for tt in range(16):
                p.dma("sync", xtile[:], x_d[brow + tt * 128: brow + (tt + 1) * 128, :],
                      waits=[("c", p.cur("c")), ("mmq", p.cur("mmq")), ("stF", p.cur("stF"))], inc="ld")
                p.do("vector", lambda e: e.engine_nop(), waits=["ld"], inc="c")
                emit_layernorm(p, sb, xtile, xln, tmp, cols, "c")
                p.do("vector", lambda e: e.tensor_copy(xbf[:], xln[:]), waits=["c"], inc="c")
                for dc in range(8):
                    p.do("tensor", (lambda dc=dc: (lambda e: nc.tensor.transpose(
                        ps_t[:], xbf[:, dc * 128:(dc + 1) * 128], ident[:])))(),
                        waits=["c", "evt"], inc="tt")
                    p.do("vector", (lambda dc=dc, tt=tt: (lambda e: e.tensor_copy(
                        AR1[:, dc * 2048 + tt * 128: dc * 2048 + tt * 128 + 128], ps_t[:])))(),
                        waits=["tt"], inc="evt")

            for which in range(3):
                for dc in range(8):
                    p.dma("gpsimd", AR6[:, dc * 1024:(dc + 1) * 1024],
                          qkvw_d[dc * 128:(dc + 1) * 128, which * 1024:(which + 1) * 1024],
                          waits=[("mmq", p.cur("mmq"))], inc="ldw")
                if which < 2:
                    dst = AR2 if which == 0 else AR3
                    for mc in range(8):
                        for tb4 in range(4):
                            for dc in range(8):
                                p.do("tensor", (lambda dc=dc, mc=mc, tb4=tb4: (lambda e: nc.tensor.matmul(
                                    ps_s[:], AR6[:, dc * 1024 + mc * 128: dc * 1024 + mc * 128 + 128],
                                    AR1[:, dc * 2048 + tb4 * 512: dc * 2048 + tb4 * 512 + 512],
                                    start=(dc == 0), stop=(dc == 7))))(),
                                    waits=(["ldw", "evt", "mmq"] if dc == 0 else []),
                                    inc=("mm" if dc == 7 else None))
                            if which == 0:
                                p.do("vector", (lambda mc=mc, tb4=tb4: (lambda e: e.tensor_scalar(
                                    AR2[:, mc * 2048 + tb4 * 512: mc * 2048 + tb4 * 512 + 512],
                                    ps_s[:], 0.125, qb8[:, mc:mc + 1], ALU.mult, ALU.add)))(),
                                    waits=["mm"], inc="mmq")
                            else:
                                p.do("vector", (lambda mc=mc, tb4=tb4: (lambda e: e.tensor_scalar_add(
                                    AR3[:, mc * 2048 + tb4 * 512: mc * 2048 + tb4 * 512 + 512],
                                    ps_s[:], qkvbc[:, 8 + mc:9 + mc])))(),
                                    waits=["mm"], inc="mmq")
                else:
                    for tt in range(16):
                        for mh in range(2):
                            for dc in range(8):
                                p.do("tensor", (lambda dc=dc, tt=tt, mh=mh: (lambda e: nc.tensor.matmul(
                                    ps_s[:], AR1[:, dc * 2048 + tt * 128: dc * 2048 + tt * 128 + 128],
                                    AR6[:, dc * 1024 + mh * 512: dc * 1024 + mh * 512 + 512],
                                    start=(dc == 0), stop=False)))(),
                                    waits=(["ldw", "mmq"] if dc == 0 else []), inc=None)
                            p.do("tensor", (lambda mh=mh: (lambda e: nc.tensor.matmul(
                                ps_s[:], onesr_bf[:], vbr[:, mh * 512: mh * 512 + 512],
                                start=False, stop=True)))(), waits=[], inc="mm")
                            p.do("vector", (lambda tt=tt, mh=mh: (lambda e: e.tensor_copy(
                                AR4[:, tt * 1024 + mh * 512: tt * 1024 + mh * 512 + 512], ps_s[:])))(),
                                waits=["mm"], inc="mmq")

            for h in range(16):
                hb = (h % 2) * 64
                hc = h // 2
                for qb in range(S // QW):
                    q0 = qb * QW
                    nkt = qb + 1
                    for kt in range(nkt):
                        k0 = kt * 128
                        p.do("tensor", (lambda hb=hb, hc=hc, kt=kt, q0=q0: (lambda e: nc.tensor.matmul(
                            ps_s[:, 0:QW],
                            AR3[hb:hb + 64, hc * 2048 + kt * 128: hc * 2048 + kt * 128 + 128],
                            AR2[hb:hb + 64, hc * 2048 + q0: hc * 2048 + q0 + QW],
                            start=True, stop=True)))(),
                            waits=["c", "mmq", ("sevt", p.cur("sevt"))], inc="mm")
                        p.do("vector", (lambda kt=kt: (lambda e: e.tensor_copy(
                            AR5[:, kt * QW:(kt + 1) * QW], ps_s[:, 0:QW])))(),
                            waits=["mm"], inc="sevt")
                        if k0 + 127 >= q0:
                            p.do("vector", (lambda kt=kt: (lambda e: e.tensor_add(
                                AR5[:, kt * QW:(kt + 1) * QW],
                                AR5[:, kt * QW:(kt + 1) * QW], cmask[:])))(),
                                waits=["sevt", "ld"], inc="sevt")
                        p.do("vector", (lambda kt=kt: (lambda e: e.reduce_max(
                            maxcols[:, kt:kt + 1], AR5[:, kt * QW:(kt + 1) * QW], axis=AX.X)))(),
                            waits=["sevt"], inc="sevt")
                    p.do("vector", (lambda nkt=nkt: (lambda e: e.reduce_max(
                        colmax[:], maxcols[:, 0:nkt], axis=AX.X)))(), waits=["sevt"], inc="c")
                    p.do("tensor", lambda e: nc.tensor.transpose(
                        ps_r[0:1, 0:128], colmax[:], identf[:]), waits=["c"], inc="mmx")
                    p.do("vector", lambda e: e.reduce_max(m11[:], ps_r[0:1, 0:128], axis=AX.X),
                         waits=["mmx"], inc="c")
                    p.do("tensor", lambda e: nc.tensor.matmul(
                        ps_bc[:, 0:1], negones_f[:], m11[:], start=True, stop=True),
                        waits=["c"], inc="mmx")
                    p.do("vector", lambda e: e.tensor_copy(negm_sb[:], ps_bc[:, 0:1]), waits=["mmx"], inc="c")
                    for kt in range(nkt):
                        p.do("scalar", (lambda kt=kt: (lambda e: e.activation(
                            ebf[:, kt * QW:(kt + 1) * QW], AR5[:, kt * QW:(kt + 1) * QW],
                            AF.Exp, bias=negm_sb[:])))(), waits=["c"], inc="eexp")
                    for kt in range(nkt):
                        p.do("tensor", (lambda kt=kt: (lambda e: nc.tensor.matmul(
                            ps_r[0:1, 0:QW], onesc_bf[:], ebf[:, kt * QW:(kt + 1) * QW],
                            start=(kt == 0), stop=(kt == nkt - 1))))(),
                            waits=(["eexp"] if kt == 0 else []), inc=("mmr" if kt == nkt - 1 else None))
                    for kt in range(nkt):
                        p.do("tensor", (lambda kt=kt, h=h: (lambda e: nc.tensor.matmul(
                            ps_c[0:64, 0:QW], AR4[:, kt * 1024 + h * 64: kt * 1024 + h * 64 + 64],
                            ebf[:, kt * QW:(kt + 1) * QW],
                            start=(kt == 0), stop=(kt == nkt - 1))))(),
                            waits=(["eexp", ("cevt", p.cur("cevt"))] if kt == 0 else []),
                            inc=("mmc" if kt == nkt - 1 else None))
                    p.do("vector", lambda e: e.reciprocal(recr[:], ps_r[0:1, 0:QW]), waits=["mmr"], inc="c")
                    p.do("tensor", lambda e: nc.tensor.matmul(
                        ps_bc[:, 0:QW], ones_f[:], recr[:], start=True, stop=True),
                        waits=["c"], inc="mmx")
                    p.do("vector", lambda e: e.tensor_copy(ctxs[:], ps_c[0:64, 0:QW]), waits=["mmc"], inc="cevt")
                    p.do("vector", (lambda h=h, hb=hb, hc=hc, q0=q0: (lambda e: e.tensor_mul(
                        AR6[hb:hb + 64, hc * 2048 + q0: hc * 2048 + q0 + QW],
                        ctxs[:], ps_bc[0:64, 0:QW])))(), waits=["mmx", "cevt"], inc="c")
                    if dbg and b == 0 and h == 0 and qb == 2:
                        p.dma("gpsimd", dbg_s, AR5[:], waits=[("c", p.cur("c"))], inc="st_dbg")
                        p.dma("gpsimd", dbg_e, ebf[:], waits=[("c", p.cur("c"))], inc="st_dbg")
                        p.dma("gpsimd", dbg_nm, negm_sb[:], waits=[("c", p.cur("c"))], inc="st_dbg")

            if dbg:
                p.dma("gpsimd", dbg_attn[b * 128:(b + 1) * 128, :], AR6[:],
                      waits=[("c", p.cur("c"))], inc="st_dbg")
                p.dma("gpsimd", dbg_q[b * 128:(b + 1) * 128, :], AR2[:],
                      waits=[("c", p.cur("c"))], inc="st_dbg")
                p.dma("gpsimd", dbg_k[b * 128:(b + 1) * 128, :], AR3[:],
                      waits=[("c", p.cur("c"))], inc="st_dbg")
                p.dma("gpsimd", dbg_v[b * 128:(b + 1) * 128, :], AR4[:],
                      waits=[("c", p.cur("c"))], inc="st_dbg")
            for dc in range(8):
                p.dma("gpsimd", AR3[:, dc * 1024:(dc + 1) * 1024],
                      outw_d[dc * 128:(dc + 1) * 128, :], waits=[("c", p.cur("c"))], inc="ldw")
            for tt in range(16):
                p.dma("sync", xtile[:], x_d[brow + tt * 128: brow + (tt + 1) * 128, :],
                      waits=[("c", p.cur("c")), ("stF", p.cur("stF"))], inc="ld")
                for mh in range(2):
                    pso = ps_h if mh == 0 else ps_o
                    for dc in range(8):
                        p.do("tensor", (lambda dc=dc, tt=tt, mh=mh, pso=pso: (lambda e: nc.tensor.matmul(
                            pso[:], AR6[:, dc * 2048 + tt * 128: dc * 2048 + tt * 128 + 128],
                            AR3[:, dc * 1024 + mh * 512: dc * 1024 + mh * 512 + 512],
                            start=(dc == 0), stop=False)))(),
                            waits=(["ldw", "c", ("yevt", p.cur("yevt"))] if dc == 0 else []), inc=None)
                    p.do("tensor", (lambda mh=mh, pso=pso: (lambda e: nc.tensor.matmul(
                        pso[:], onesr_bf[:], outbr[:, mh * 512: mh * 512 + 512],
                        start=False, stop=True)))(), waits=[], inc="mm")
                p.do("vector", lambda e: e.engine_nop(), waits=["ld"], inc="c")
                emit_layernorm(p, sb, xtile, xln, tmp, cols, "c")
                p.do("vector", lambda e: e.tensor_add(xtile[:, 0:512], xln[:, 0:512], ps_h[:]), waits=["c", "mm"], inc="yevt")
                p.do("vector", lambda e: e.tensor_add(xtile[:, 512:1024], xln[:, 512:1024], ps_o[:]), waits=["c"], inc="yevt")
                emit_layernorm(p, sb, xtile, xln, tmp, cols, "c")
                p.do("vector", lambda e: e.tensor_copy(xbf[:], xln[:]), waits=["c"], inc="c")
                for dc in range(8):
                    p.do("tensor", (lambda dc=dc: (lambda e: nc.tensor.transpose(
                        ps_t[:], xbf[:, dc * 128:(dc + 1) * 128], ident[:])))(),
                        waits=["c", "evt"], inc="tt")
                    p.do("vector", (lambda dc=dc, tt=tt: (lambda e: e.tensor_scalar(
                        AR1[:, dc * 2048 + tt * 128: dc * 2048 + tt * 128 + 128], ps_t[:],
                        n1c[:, dc:dc + 1], n1c[:, 8 + dc:9 + dc], ALU.mult, ALU.add)))(),
                        waits=["tt"], inc="evt")
                for dc in range(8):
                    p.do("tensor", (lambda dc=dc: (lambda e: nc.tensor.transpose(
                        ps_s[:, 0:128], xln[:, dc * 128:(dc + 1) * 128], identf[:])))(),
                        waits=["c", "evt", ("xtF", p.cur("xtF"))], inc="mmxt")
                    p.do("vector", (lambda dc=dc: (lambda e: e.tensor_scalar(
                        xTf[:], ps_s[:, 0:128],
                        n1c[:, dc:dc + 1], n1c[:, 8 + dc:9 + dc], ALU.mult, ALU.add)))(),
                        waits=["mmxt"], inc="xtF")
                    p.do("tensor", (lambda dc=dc, tt=tt: (lambda e: nc.tensor.matmul(
                        ps_g[:, 0:8], xTf[:], centTf[:, dc * 8:(dc + 1) * 8],
                        start=(dc == 0), stop=(dc == 7))))(),
                        waits=["xtF"], inc=("mmg" if dc == 7 else "mmxt"))
                p.do("vector", (lambda tt=tt: (lambda e: e.tensor_copy(
                    logits_all[:, tt * 8:(tt + 1) * 8], ps_g[:, 0:8])))(),
                    waits=["mmg"], inc="c")

            if dbg:
                p.dma("gpsimd", dbg_x1[b * 128:(b + 1) * 128, :], AR1[:],
                      waits=[("evt", p.cur("evt"))], inc="st_dbg")
            for tt in range(16):
                p.do("vector", (lambda tt=tt: (lambda e: e.tensor_copy(
                    logits[:], logits_all[:, tt * 8:(tt + 1) * 8])))(), waits=["c"], inc="c")
                p.do("vector", lambda e: e.reduce_max(lmax[:], logits[:], axis=AX.X, negate=True), waits=["c"], inc="c")
                p.do("scalar", lambda e: e.activation(sim[:], logits[:], AF.Exp, bias=lmax[:]), waits=["c"], inc="c")
                p.do("vector", lambda e: e.reduce_sum(lsum[:], sim[:], axis=AX.X), waits=["c"], inc="c")
                p.do("vector", lambda e: e.reciprocal(lsum[:], lsum[:]), waits=["c"], inc="c")
                p.do("vector", lambda e: e.tensor_scalar_mul(sim[:], sim[:], lsum[:]), waits=["c"], inc="c")
                p.do("tensor", lambda e: nc.tensor.matmul(
                    ps_g[:, 8:16], ones_f[:], rbrow[:], start=True, stop=True),
                    waits=["c"], inc="mmg")
                p.do("vector", lambda e: e.tensor_add(vb[:], sim[:], ps_g[:, 8:16]), waits=["mmg"], inc="c")
                p.do("vector", lambda e: e.reduce_max(lmax[:], vb[:], axis=AX.X), waits=["c"], inc="c")
                p.do("vector", lambda e: e.tensor_single_scalar(mk1[:], vb[:], lmax[:], ALU.is_equal), waits=["c"], inc="c")
                p.do("vector", lambda e: e.scalar_tensor_tensor(vb2[:], mk1[:], -1e30, vb[:], ALU.mult, ALU.add), waits=["c"], inc="c")
                p.do("vector", lambda e: e.reduce_max(lmax[:], vb2[:], axis=AX.X), waits=["c"], inc="c")
                p.do("vector", lambda e: e.tensor_single_scalar(mk2[:], vb2[:], lmax[:], ALU.is_equal), waits=["c"], inc="c")
                p.do("vector", lambda e: e.tensor_add(mk1[:], mk1[:], mk2[:]), waits=["c"], inc="c")
                p.do("vector", lambda e: e.tensor_mul(sim[:], sim[:], mk1[:]), waits=["c"], inc="c")
                p.do("vector", lambda e: e.reduce_sum(gsum[:], sim[:], axis=AX.X), waits=["c"], inc="c")
                p.do("vector", lambda e: e.reciprocal(gsum[:], gsum[:]), waits=["c"], inc="c")
                p.do("vector", lambda e: e.tensor_scalar_mul(gates_bf[:], sim[:], gsum[:]), waits=["c"], inc="c")
                p.do("tensor", lambda e: nc.tensor.transpose(ps_t[0:8, :], gates_bf[:], ident[:]),
                     waits=["c", ("gevt", p.cur("gevt"))], inc="mmg")
                p.do("vector", (lambda tt=tt: (lambda e: e.tensor_copy(
                    gatesT[:, tt * 128:(tt + 1) * 128], ps_t[0:8, :])))(), waits=["mmg"], inc="gevt")

            if dbg:
                p.dma("gpsimd", dbg_gat[b * 8:(b + 1) * 8, :], gatesT[:],
                      waits=[("gevt", p.cur("gevt"))], inc="st_dbg")
            for tb in range(S // TB):
                t0 = tb * TB
                for mc in range(8):
                    for th in range(2):
                        p.do("vector", (lambda mc=mc, th=th, t0=t0: (lambda e: e.tensor_copy(
                            ACC[:, mc * 1024 + th * 512: mc * 1024 + th * 512 + 512],
                            AR1[:, mc * 2048 + t0 + th * 512: mc * 2048 + t0 + th * 512 + 512])))(),
                            waits=[("c", p.cur("c")), ("mmo2", p.cur("mmo2"))], inc="c")
                for ex in range(9):
                    for dc in range(8):
                        p.dma("gpsimd", AR3[:, dc * 2048:(dc + 1) * 2048],
                              w1all_d[ex * 1024 + dc * 128: ex * 1024 + dc * 128 + 128, :],
                              waits=[("mmh", p.cur("mmh")), ("mmo2", p.cur("mmo2")),
                                     ("yevt", p.cur("yevt"))], inc="ldw")
                    for ffc in range(16):
                        for th in range(2):
                            for dc in range(8):
                                p.do("tensor", (lambda dc=dc, ffc=ffc, th=th, t0=t0: (lambda e: nc.tensor.matmul(
                                    ps_h[:], AR3[:, dc * 2048 + ffc * 128: dc * 2048 + ffc * 128 + 128],
                                    AR1[:, dc * 2048 + t0 + th * 512: dc * 2048 + t0 + th * 512 + 512],
                                    start=(dc == 0), stop=(dc == 7))))(),
                                    waits=(["ldw", ("hevt", p.cur("hevt")),
                                            ("mmo2", p.cur("mmo2")),
                                            ("statF", p.cur("statF"))] if dc == 0 else []),
                                    inc=("mmh" if dc == 7 else None))
                            p.do("scalar", (lambda ffc=ffc, th=th, ex=ex: (lambda e: e.activation(
                                AR2[:, ffc * 1024 + th * 512: ffc * 1024 + th * 512 + 512],
                                ps_h[:], AF.Gelu, bias=b1c[:, ex * 16 + ffc: ex * 16 + ffc + 1])))(),
                                waits=["mmh"], inc="hevt")
                    for fc in range(16):
                        p.dma("gpsimd", AR3[:, fc * 1024:(fc + 1) * 1024],
                              w2all_d[ex * 2048 + fc * 128: ex * 2048 + fc * 128 + 128, :],
                              waits=[("mmh", p.cur("mmh")), ("mmo2", p.cur("mmo2"))], inc="ldw")
                    for th in range(2):
                        if ex > 0:
                            p.do("tensor", (lambda ex=ex, t0=t0, th=th: (lambda e: nc.tensor.matmul(
                                ps_g[:], sel8[:, (ex - 1) * 128: ex * 128],
                                gatesT[:, t0 + th * 512: t0 + th * 512 + 512],
                                start=True, stop=True)))(),
                                waits=["gevt", "hevt", ("gbF", p.cur("gbF"))], inc="mmg2")
                            p.do("vector", lambda e: e.tensor_copy(gb_sb[:], ps_g[:]), waits=["mmg2"], inc="gbF")
                        for mc in range(8):
                            for fc in range(16):
                                p.do("tensor", (lambda fc=fc, mc=mc, th=th: (lambda e: nc.tensor.matmul(
                                    ps_o[:], AR3[:, fc * 1024 + mc * 128: fc * 1024 + mc * 128 + 128],
                                    AR2[:, fc * 1024 + th * 512: fc * 1024 + th * 512 + 512],
                                    start=(fc == 0), stop=(fc == 15))))(),
                                    waits=(["ldw", "hevt", ("accF", p.cur("accF"))] if fc == 0 else []),
                                    inc=("mmo2" if fc == 15 else None))
                            if ex == 0:
                                p.do("vector", (lambda mc=mc, th=th, ex=ex: (lambda e: e.scalar_tensor_tensor(
                                    ACC[:, mc * 1024 + th * 512: mc * 1024 + th * 512 + 512],
                                    ps_o[:], b2c[:, ex * 8 + mc: ex * 8 + mc + 1],
                                    ACC[:, mc * 1024 + th * 512: mc * 1024 + th * 512 + 512],
                                    ALU.add, ALU.add)))(), waits=["mmo2"], inc="accF")
                            else:
                                p.do("vector", (lambda mc=mc, th=th, ex=ex: (lambda e: e.scalar_tensor_tensor(
                                    t1_sb[:], ps_o[:], b2c[:, ex * 8 + mc: ex * 8 + mc + 1],
                                    gb_sb[:], ALU.add, ALU.mult)))(), waits=["mmo2", "gbF"], inc="c")
                                p.do("vector", (lambda mc=mc, th=th: (lambda e: e.tensor_add(
                                    ACC[:, mc * 1024 + th * 512: mc * 1024 + th * 512 + 512],
                                    ACC[:, mc * 1024 + th * 512: mc * 1024 + th * 512 + 512],
                                    t1_sb[:])))(), waits=["c"], inc="accF")
                if dbg and b == 0 and tb == 1:
                    p.dma("gpsimd", dbg_acc, ACC[:], waits=[("accF", p.cur("accF"))], inc="st_dbg")
                for ts2 in range(TB // 128):
                    tg = t0 + ts2 * 128
                    for mc in range(8):
                        p.do("scalar", (lambda mc=mc, ts2=ts2: (lambda e: e.activation(
                            sqs[:, 0:128], ACC[:, mc * 1024 + ts2 * 128: mc * 1024 + ts2 * 128 + 128],
                            AF.Square)))(), waits=["accF", ("c", p.cur("c")), ("mmsq", p.cur("mmsq"))], inc="c")
                        p.do("tensor", (lambda mc=mc, ts2=ts2: (lambda e: nc.tensor.matmul(
                            ps_c[:, 0:1], ACC[:, mc * 1024 + ts2 * 128: mc * 1024 + ts2 * 128 + 128],
                            onescf[:], start=(mc == 0), stop=(mc == 7))))(),
                            waits=(["c", ("statF", p.cur("statF"))] if mc == 0 else []),
                            inc=("mmst" if mc == 7 else None))
                        p.do("tensor", (lambda mc=mc: (lambda e: nc.tensor.matmul(
                            ps_h[:, 0:1], sqs[:, 0:128], onescf[:], start=(mc == 0), stop=(mc == 7))))(),
                            waits=["c"], inc="mmsq")
                    p.do("vector", lambda e: e.tensor_scalar_mul(statm[:], ps_c[:, 0:1], 1.0 / 1024), waits=["mmst"], inc="c")
                    p.do("vector", lambda e: e.tensor_mul(statm2[:], statm[:], statm[:]), waits=["c"], inc="c")
                    p.do("vector", lambda e: e.tensor_scalar_mul(statv[:], ps_h[:, 0:1], 1.0 / 1024), waits=["c", ("mmsq", p.cur("mmsq"))], inc="statF")
                    p.do("vector", lambda e: e.tensor_sub(statv[:], statv[:], statm2[:]), waits=["c", "statF"], inc="c")
                    p.do("vector", lambda e: e.tensor_scalar_add(statv[:], statv[:], 1e-5), waits=["c"], inc="c")
                    p.do("scalar", lambda e: e.activation(statv[:], statv[:], AF.Sqrt), waits=["c"], inc="c")
                    p.do("vector", lambda e: e.reciprocal(statv[:], statv[:]), waits=["c"], inc="c")
                    p.do("tensor", lambda e: nc.tensor.transpose(ps_r[0:1, 0:128], statm[:], identf[:]),
                         waits=["c"], inc="mmst")
                    p.do("vector", lambda e: e.tensor_copy(ctxs[0:1, 0:128], ps_r[0:1, 0:128]), waits=["mmst"], inc="c")
                    p.do("tensor", lambda e: nc.tensor.transpose(ps_r[0:1, 0:128], statv[:], identf[:]),
                         waits=["c"], inc="mmst")
                    p.do("vector", lambda e: e.tensor_copy(recr[0:1, 0:128], ps_r[0:1, 0:128]), waits=["mmst"], inc="c")
                    p.do("tensor", lambda e: nc.tensor.matmul(
                        ps_bc[:, 0:128], ones_f[:], ctxs[0:1, 0:128], start=True, stop=True), waits=["c"], inc="mmst")
                    p.do("tensor", lambda e: nc.tensor.matmul(
                        ps_bc[:, 128:256], ones_f[:], recr[0:1, 0:128], start=True, stop=True), waits=["c"], inc="mmst")
                    p.dma("sync", xtile[:], x_d[brow + tg: brow + tg + 128, :],
                          waits=[("c", p.cur("c")), ("stF", p.cur("stF"))], inc="ld")
                    for mc in range(8):
                        p.do("vector", (lambda mc=mc, ts2=ts2: (lambda e: e.tensor_sub(
                            sqs[:, 0:128], ACC[:, mc * 1024 + ts2 * 128: mc * 1024 + ts2 * 128 + 128],
                            ps_bc[:, 0:128])))(), waits=["mmst", "c", ("c", p.cur("c"))], inc="c")
                        p.do("vector", lambda e: e.tensor_mul(sqs[:, 128:256], sqs[:, 0:128], ps_bc[:, 128:256]), waits=["c"], inc="c")
                        p.do("vector", (lambda mc=mc: (lambda e: e.tensor_scalar(
                            xbf[:, 0:128], sqs[:, 128:256], n2c[:, mc:mc + 1], n2c[:, 8 + mc:9 + mc],
                            ALU.mult, ALU.add)))(), waits=["c"], inc="c")
                        p.do("tensor", lambda e: nc.tensor.transpose(ps_t[:], xbf[:, 0:128], ident[:]),
                             waits=["c", ("oevt", p.cur("oevt"))], inc="mmst")
                        p.do("vector", (lambda mc=mc: (lambda e: e.tensor_add(
                            xtile[:, mc * 128:(mc + 1) * 128], ps_t[:], xtile[:, mc * 128:(mc + 1) * 128])))(),
                            waits=["mmst", "ld"], inc="oevt")
                    p.dma("gpsimd", y_d[brow + tg: brow + tg + 128, :], xtile[:],
                          waits=[("oevt", p.cur("oevt"))], inc="stF")
        p.do("gpsimd", lambda e: e.engine_nop(), waits=["stF"], inc=None)


def prep_ffn_weights(d):
    import ml_dtypes
    out = {}
    out["qkvw_dr"] = np.ascontiguousarray(d["L0_ffn_qkv_w"])
    out["qkvbc_dr"] = np.ascontiguousarray(d["L0_ffn_qkv_b"].reshape(24, 128).T)
    out["vbr_dr"] = np.ascontiguousarray(d["L0_ffn_qkv_b"][2048:].reshape(1, 1024))
    out["outw_dr"] = np.ascontiguousarray(d["L0_ffn_out_w"])
    out["outbr_dr"] = np.ascontiguousarray(d["L0_ffn_out_b"].reshape(1, 1024))
    out["n1_dr"] = np.ascontiguousarray(np.concatenate(
        [d["L0_ffn_norm1_w"].reshape(8, 128).T, d["L0_ffn_norm1_b"].reshape(8, 128).T], axis=1))
    out["n2_dr"] = np.ascontiguousarray(np.concatenate(
        [d["L0_ffn_norm2_w"].reshape(8, 128).T, d["L0_ffn_norm2_b"].reshape(8, 128).T], axis=1))
    w1 = np.concatenate([d["L0_ffn_sw1"], d["L0_ffn_rw1"]], axis=0)
    w2 = np.concatenate([d["L0_ffn_sw2"], d["L0_ffn_rw2"]], axis=0)
    b1 = np.concatenate([d["L0_ffn_sb1"], d["L0_ffn_rb1"]], axis=0)
    b2 = np.concatenate([d["L0_ffn_sb2"], d["L0_ffn_rb2"]], axis=0)
    out["w1all_dr"] = np.ascontiguousarray(w1.reshape(9 * 1024, 2048))
    out["w2all_dr"] = np.ascontiguousarray(w2.reshape(9 * 2048, 1024))
    out["b1c_dr"] = np.ascontiguousarray(b1.reshape(9, 16, 128).transpose(2, 0, 1).reshape(128, 144))
    out["b2c_dr"] = np.ascontiguousarray(b2.reshape(9, 8, 128).transpose(2, 0, 1).reshape(128, 72))
    out["centT_dr"] = np.ascontiguousarray(d["L0_ffn_centroid"].T)
    out["rb_dr"] = np.ascontiguousarray(d["L0_ffn_route_bias"].reshape(1, 8))
    sel = np.zeros((8, 8, 128), np.float32)
    for n in range(8):
        sel[n, n, :] = 1.0
    out["sel8_dr"] = np.ascontiguousarray(sel.transpose(1, 0, 2).reshape(8, 1024)).astype(ml_dtypes.bfloat16)
    out["idmat"] = np.eye(128).astype(ml_dtypes.bfloat16)
    ii = np.arange(128)[:, None]; jj = np.arange(128)[None, :]
    out["cmaskf_dr"] = np.where(jj >= ii, 0.0, -1e30).astype(np.float32)
    return out





def emit_blockA(nc, p, DD, x_d, y_d):
    rel_d = DD["rel_dr"]
    wq_d = DD["wq_dr"]
    wk_d = DD["wk_dr"]
    wv_d = DD["wv_dr"]
    wpk_d = DD["wpk_dr"]
    wpq_d = DD["wpq_dr"]
    bqkv_d = DD["bqkv_dr"]
    bvr_d = DD["bvr_dr"]
    bpq_d = DD["bpq_dr"]
    id_d = DD["idmat"]
    idf_d = DD["idmatf"]
    mge_d = DD["mge_dr"]
    mlt_d = DD["mlt_dr"]

    poskD = nc.dram_tensor("poskD", [1024, 1024], BF16, kind="Internal").ap()
    posqD = nc.dram_tensor("posqD", [1024, 1024], BF16, kind="Internal").ap()
    c2pR = nc.dram_tensor("c2pR", [2052, 1024], BF16, kind="Internal").ap()
    p2cA = nc.dram_tensor("p2cA", [2052, 1024], BF16, kind="Internal").ap()

    ctx = ExitStack()
    with ctx:
        sb = lambda name, shp, dt=F32: ctx.enter_context(nc.sbuf_tensor("A" + name, shp, dt))
        ps = lambda name, shp, dt=F32: ctx.enter_context(nc.psum_tensor("A" + name, shp, dt))

        AR1 = sb("AR1", [128, 16384], BF16)
        AR2 = sb("AR2", [128, 16384], BF16)
        AR3 = sb("AR3", [128, 16384], BF16)
        AR4 = sb("AR4", [128, 16384], BF16)
        AR6 = sb("AR6", [128, 16384], BF16)
        AR5 = sb("AR5", [128, 2048])
        ebf = sb("ebf", [128, 2048], BF16)
        c2pr_sb = sb("c2pr_sb", [128, 512], BF16)
        p2ca_sb = sb("p2ca_sb", [128, 512], BF16)
        poskh = sb("poskh", [128, 1024], BF16)
        posqh = sb("posqh", [128, 1024], BF16)
        ident = sb("ident", [128, 128], BF16)
        identf = sb("identf", [128, 128], F32)
        mge = sb("mge", [128, 128], F32)
        mlt = sb("mlt", [128, 128], F32)
        bqkv = sb("bqkv", [128, 24])
        qb8 = sb("qb8", [128, 8])
        bvr = sb("bvr", [1, 1024], BF16)
        bpqc = sb("bpqc", [128, 8])
        onesc_bf = sb("onesc_bf", [128, 1], BF16)
        onesr_bf = sb("onesr_bf", [1, 128], BF16)
        ones_f = sb("ones_f", [1, 128])
        negones_f = sb("negones_f", [1, 128])
        maxcols = sb("maxcols", [128, 16])
        colmax = sb("colmax", [128, 1])
        negm_sb = sb("negm_sb", [128, 1])
        m11 = sb("m11", [1, 1])
        recr = sb("recr", [1, 128])
        ctxs = sb("ctxs", [64, 128])
        xtile = sb("xtile", [128, D])
        xln = sb("xln", [128, D]); tmp = sb("tmp", [128, D])
        xbf = sb("xbf", [128, D], BF16)
        cols = (sb("c_m", [128, 1]), sb("c_v", [128, 1]), sb("c_r", [128, 1]))
        pconsts = sb("pconsts", [128, 32])
        cc_lo = sb("cc_lo", [128, 128]); cc_hi = sb("cc_hi", [128, 128])
        g_t = sb("g_t", [128, 128]); g2_t = sb("g2_t", [128, 128])
        t1_t = sb("t1_t", [128, 128])
        cg_bf = sb("cg_bf", [128, 128], BF16)
        pg_bf = sb("pg_bf", [128, 128], BF16)

        ps_s = ps("ps_s", [128, 512])
        ps_t = ps("ps_t", [128, 128], BF16)
        ps_r = ps("ps_r", [1, 256])
        ps_c = ps("ps_c", [128, 256])
        ps_bc = ps("ps_bc", [128, 256])

        SQ192 = float(1.0 / (8.0 * np.sqrt(np.float32(192.0))))

        for dst, srcd in [(ident, id_d), (identf, idf_d), (mge, mge_d), (mlt, mlt_d),
                          (bqkv, bqkv_d), (bpqc, bpq_d)]:
            p.dma("sync", dst[:], srcd, inc="ld")
        p.dma("gpsimd", bvr[:], bvr_d, inc="ldw")
        p.do("vector", lambda e: e.memset(onesc_bf[:], 1.0), inc="c")
        p.do("vector", lambda e: e.memset(onesr_bf[:], 1.0), inc="c")
        p.do("vector", lambda e: e.memset(ones_f[:], 1.0), inc="c")
        p.do("vector", lambda e: e.memset(negones_f[:], -1.0), inc="c")
        p.do("vector", lambda e: e.memset(ebf[0:2, 0:1024], 0.0), inc="c")
        p.do("vector", lambda e: e.tensor_scalar_mul(qb8[:], bqkv[:, 0:8], 0.125), waits=["ld"], inc="c")
        for dd in (c2pR, p2cA):
            p.dma("gpsimd", dd[0:2, :], ebf[0:2, 0:1024], waits=[("c", p.cur("c"))], inc="ldw")
            p.dma("gpsimd", dd[2050:2052, :], ebf[0:2, 0:1024], waits=[("c", p.cur("c"))], inc="ldw")

        for rt in range(8):
            p.dma("sync", xtile[:], rel_d[rt * 128:(rt + 1) * 128, :],
                  waits=[("c", p.cur("c"))], inc="ld")
            p.do("vector", lambda e: e.engine_nop(), waits=["ld"], inc="c")
            p.do("vector", lambda e: e.tensor_copy(xbf[:], xtile[:]), waits=["c"], inc="c")
            for ec in range(8):
                p.do("tensor", (lambda ec=ec: (lambda e: nc.tensor.transpose(
                    ps_t[:], xbf[:, ec * 128:(ec + 1) * 128], ident[:])))(),
                    waits=["c", "evt"], inc="tt")
                p.do("vector", (lambda ec=ec, rt=rt: (lambda e: e.tensor_copy(
                    AR6[:, ec * 1024 + rt * 128: ec * 1024 + rt * 128 + 128], ps_t[:])))(),
                    waits=["tt"], inc="evt")
        for scaled, wd, dst in [(False, wpk_d, poskD), (True, wpq_d, posqD)]:
            for dc in range(8):
                p.dma("gpsimd", AR6[:, 8192 + dc * 1024: 8192 + (dc + 1) * 1024],
                      wd[dc * 128:(dc + 1) * 128, :], waits=[("mmw", p.cur("mmw"))], inc="ldw")
            for mc in range(8):
                for rb in range(2):
                    for ec in range(8):
                        p.do("tensor", (lambda ec=ec, mc=mc, rb=rb: (lambda e: nc.tensor.matmul(
                            ps_s[:], AR6[:, 8192 + ec * 1024 + mc * 128: 8192 + ec * 1024 + mc * 128 + 128],
                            AR6[:, ec * 1024 + rb * 512: ec * 1024 + rb * 512 + 512],
                            start=(ec == 0), stop=(ec == 7))))(),
                            waits=(["ldw", "evt", ("posF", p.cur("posF"))] if ec == 0 else []),
                            inc=("mmw" if ec == 7 else None))
                    if scaled:
                        p.do("vector", (lambda mc=mc: (lambda e: e.tensor_scalar(
                            c2pr_sb[:, 0:512], ps_s[:], bpqc[:, mc:mc + 1], SQ192,
                            ALU.add, ALU.mult)))(), waits=["mmw"], inc="posF")
                    else:
                        p.do("vector", lambda e: e.tensor_copy(c2pr_sb[:, 0:512], ps_s[:]),
                             waits=["mmw"], inc="posF")
                    p.dma("gpsimd", dst[mc * 128:(mc + 1) * 128, rb * 512:(rb + 1) * 512],
                          c2pr_sb[:, 0:512], waits=[("posF", p.cur("posF"))], inc="stpos")

        for b in range(B_CORE):
            brow = b * S
            for tt in range(16):
                p.dma("sync", xtile[:], x_d[brow + tt * 128: brow + (tt + 1) * 128, :],
                      waits=[("c", p.cur("c")), ("mmq", p.cur("mmq")), ("stF", p.cur("stF"))], inc="ld")
                p.do("vector", lambda e: e.engine_nop(), waits=["ld"], inc="c")
                emit_layernorm(p, sb, xtile, xln, tmp, cols, "c")
                p.do("vector", lambda e: e.tensor_copy(xbf[:], xln[:]), waits=["c"], inc="c")
                for dc in range(8):
                    p.do("tensor", (lambda dc=dc: (lambda e: nc.tensor.transpose(
                        ps_t[:], xbf[:, dc * 128:(dc + 1) * 128], ident[:])))(),
                        waits=["c", "evt"], inc="tt")
                    p.do("vector", (lambda dc=dc, tt=tt: (lambda e: e.tensor_copy(
                        AR1[:, dc * 2048 + tt * 128: dc * 2048 + tt * 128 + 128], ps_t[:])))(),
                        waits=["tt"], inc="evt")
            for which, wd in [(0, wq_d), (1, wk_d), (2, wv_d)]:
                for dc in range(8):
                    p.dma("gpsimd", AR6[:, 8192 + dc * 1024: 8192 + (dc + 1) * 1024],
                          wd[dc * 128:(dc + 1) * 128, :],
                          waits=[("mmq", p.cur("mmq")), ("mmw", p.cur("mmw"))], inc="ldw")
                if which < 2:
                    for mc in range(8):
                        for tb4 in range(4):
                            for dc in range(8):
                                p.do("tensor", (lambda dc=dc, mc=mc, tb4=tb4: (lambda e: nc.tensor.matmul(
                                    ps_s[:], AR6[:, 8192 + dc * 1024 + mc * 128: 8192 + dc * 1024 + mc * 128 + 128],
                                    AR1[:, dc * 2048 + tb4 * 512: dc * 2048 + tb4 * 512 + 512],
                                    start=(dc == 0), stop=(dc == 7))))(),
                                    waits=(["ldw", "evt", "mmq"] if dc == 0 else []),
                                    inc=("mm" if dc == 7 else None))
                            if which == 0:
                                p.do("vector", (lambda mc=mc, tb4=tb4: (lambda e: e.tensor_scalar(
                                    AR2[:, mc * 2048 + tb4 * 512: mc * 2048 + tb4 * 512 + 512],
                                    ps_s[:], 0.125, qb8[:, mc:mc + 1], ALU.mult, ALU.add)))(),
                                    waits=["mm"], inc="mmq")
                            else:
                                p.do("vector", (lambda mc=mc, tb4=tb4: (lambda e: e.tensor_scalar_add(
                                    AR3[:, mc * 2048 + tb4 * 512: mc * 2048 + tb4 * 512 + 512],
                                    ps_s[:], bqkv[:, 8 + mc:9 + mc])))(),
                                    waits=["mm"], inc="mmq")
                else:
                    for tt in range(16):
                        for mh in range(2):
                            for dc in range(8):
                                p.do("tensor", (lambda dc=dc, tt=tt, mh=mh: (lambda e: nc.tensor.matmul(
                                    ps_s[:], AR1[:, dc * 2048 + tt * 128: dc * 2048 + tt * 128 + 128],
                                    AR6[:, 8192 + dc * 1024 + mh * 512: 8192 + dc * 1024 + mh * 512 + 512],
                                    start=(dc == 0), stop=False)))(),
                                    waits=(["ldw", "mmq"] if dc == 0 else []), inc=None)
                            p.do("tensor", (lambda mh=mh: (lambda e: nc.tensor.matmul(
                                ps_s[:], onesr_bf[:], bvr[:, mh * 512: mh * 512 + 512],
                                start=False, stop=True)))(),
                                waits=(["ldw"] if tt == 0 and mh == 0 else []), inc="mm")
                            p.do("vector", (lambda tt=tt, mh=mh: (lambda e: e.tensor_copy(
                                AR4[:, tt * 1024 + mh * 512: tt * 1024 + mh * 512 + 512], ps_s[:])))(),
                                waits=["mm"], inc="mmq")

            for h in range(16):
                hb = (h % 2) * 64
                hc = h // 2
                p.dma("sync", poskh[hb:hb + 64, :], poskD[h * 64:(h + 1) * 64, :],
                      waits=[("cF", p.cur("cF")), ("stpos", p.cur("stpos"))], inc="ldp")
                p.dma("sync", posqh[hb:hb + 64, :], posqD[h * 64:(h + 1) * 64, :],
                      waits=[("cF", p.cur("cF")), ("stpos", p.cur("stpos"))], inc="ldp")
                for qt in range(16):
                    for rb in range(2):
                        p.do("tensor", (lambda hb=hb, hc=hc, qt=qt, rb=rb: (lambda e: nc.tensor.matmul(
                            ps_s[:], AR2[hb:hb + 64, hc * 2048 + qt * 128: hc * 2048 + qt * 128 + 128],
                            poskh[hb:hb + 64, rb * 512:(rb + 1) * 512],
                            start=True, stop=True)))(),
                            waits=(["ldp", "mmq", ("aevt", p.cur("aevt"))] if rb == 0 else ["aevt"]),
                            inc="mma")
                        p.do("vector", (lambda rb=rb: (lambda e: e.tensor_copy(
                            c2pr_sb[:], ps_s[:, 511::-1])))(),
                            waits=["mma", ("sta", p.cur("sta"))], inc="aevt")
                        p.dma("gpsimd", c2pR[2 + qt * 128: 2 + qt * 128 + 128,
                                             (1 - rb) * 512:(2 - rb) * 512],
                              c2pr_sb[:], waits=[("aevt", p.cur("aevt"))], inc="sta")
                for kt in range(16):
                    for rb in range(2):
                        p.do("tensor", (lambda hb=hb, hc=hc, kt=kt, rb=rb: (lambda e: nc.tensor.matmul(
                            ps_s[:], AR3[hb:hb + 64, hc * 2048 + kt * 128: hc * 2048 + kt * 128 + 128],
                            posqh[hb:hb + 64, rb * 512:(rb + 1) * 512],
                            start=True, stop=True)))(),
                            waits=["aevt"], inc="mma")
                        p.do("vector", (lambda rb=rb: (lambda e: e.tensor_copy(
                            p2ca_sb[:], ps_s[:])))(),
                            waits=["mma", ("sta", p.cur("sta"))], inc="aevt")
                        p.dma("gpsimd", p2cA[2 + kt * 128: 2 + kt * 128 + 128,
                                             rb * 512:(rb + 1) * 512],
                              p2ca_sb[:], waits=[("aevt", p.cur("aevt"))], inc="sta")
                for kt in range(16):
                    rhs2 = bass.AP(posqh[:].tensor, hb * 1024, [[1024, 64], [1023, 2]])
                    p.do("tensor", (lambda hb=hb, hc=hc, kt=kt, rhs2=rhs2: (lambda e: nc.tensor.matmul(
                        ps_c[:, 0:2], AR3[hb:hb + 64, hc * 2048 + kt * 128: hc * 2048 + kt * 128 + 128],
                        rhs2, start=True, stop=True)))(),
                        waits=["ldp", "mmq", ("cF", p.cur("cF"))], inc="mma")
                    p.do("vector", (lambda kt=kt: (lambda e: e.tensor_copy(
                        pconsts[:, kt * 2: kt * 2 + 2], ps_c[:, 0:2])))(), waits=["mma"], inc="cF")

                for qb in range(16):
                    q0 = qb * 128
                    for sel, cct in [(0, cc_lo), (1023, cc_hi)]:
                        p.do("tensor", (lambda hb=hb, hc=hc, q0=q0, sel=sel: (lambda e: nc.tensor.matmul(
                            ps_r[0:1, 0:128], poskh[hb:hb + 64, sel:sel + 1],
                            AR2[hb:hb + 64, hc * 2048 + q0: hc * 2048 + q0 + 128],
                            start=True, stop=True)))(),
                            waits=[("cF", p.cur("cF")), "mmq"], inc="mma")
                        p.do("vector", lambda e: e.tensor_copy(recr[:], ps_r[0:1, 0:128]),
                             waits=["mma"], inc="cF")
                        p.do("tensor", lambda e: nc.tensor.matmul(
                            ps_bc[:, 0:128], ones_f[:], recr[:], start=True, stop=True),
                            waits=["cF"], inc="mma")
                        p.do("vector", (lambda cct=cct: (lambda e: e.tensor_copy(
                            cct[:], ps_bc[:, 0:128])))(), waits=["mma"], inc="cF")
                    for kt in range(16):
                        J00 = (qb - kt) * 128 + 512
                        k0 = kt * 128
                        p.do("tensor", (lambda hb=hb, hc=hc, kt=kt, q0=q0: (lambda e: nc.tensor.matmul(
                            ps_s[:, 0:128],
                            AR3[hb:hb + 64, hc * 2048 + kt * 128: hc * 2048 + kt * 128 + 128],
                            AR2[hb:hb + 64, hc * 2048 + q0: hc * 2048 + q0 + 128],
                            start=True, stop=True)))(),
                            waits=[("sevt", p.cur("sevt")), "cF"], inc="mms")
                        if -128 < J00 < 1152:
                            c2off = (q0 + 2) * 1024 + 511 - q0 + k0
                            c2view = bass.AP(c2pR.tensor, c2off, [[1023, 128], [1, 128]])
                            p.do("sync", (lambda c2view=c2view: (lambda e: e.dma_start_transpose(
                                out=cg_bf[:], in_=c2view)))(),
                                waits=[("sta", p.cur("sta")), ("gF", p.cur("gF"))], inc="gdma", inck=16)
                            pcoff = (k0 + 2) * 1024 + q0 - k0 + 512
                            pcview = bass.AP(p2cA.tensor, pcoff, [[1023, 128], [1, 128]])
                            p.dma("sync", pg_bf[:], pcview,
                                  waits=[("sta", p.cur("sta")), ("gF", p.cur("gF"))], inc="gdma")
                        if 128 <= J00 <= 896:
                            p.do("vector", lambda e: e.tensor_add(g_t[:], cg_bf[:], pg_bf[:]),
                                 waits=["gdma"], inc="gF")
                            p.do("vector", (lambda kt=kt: (lambda e: e.tensor_add(
                                AR5[:, kt * 128:(kt + 1) * 128], ps_s[:, 0:128], g_t[:])))(),
                                waits=["mms", "gF"], inc="sevt")
                        elif J00 >= 1152 or J00 <= -128:
                            sel = 1 if J00 >= 1152 else 0
                            cct = cc_hi if sel else cc_lo
                            p.do("vector", (lambda kt=kt, cct=cct, sel=sel: (lambda e: e.scalar_tensor_tensor(
                                AR5[:, kt * 128:(kt + 1) * 128], cct[:],
                                pconsts[:, kt * 2 + sel: kt * 2 + sel + 1], ps_s[:, 0:128],
                                ALU.add, ALU.add)))(), waits=["mms", "cF"], inc="sevt")
                        else:
                            lo = (J00 == 0)
                            m_in = mge if lo else mlt
                            m_out = mlt if lo else mge
                            sel = 0 if lo else 1
                            cct = cc_lo if lo else cc_hi
                            p.do("vector", lambda e: e.tensor_add(g_t[:], cg_bf[:], pg_bf[:]),
                                 waits=["gdma"], inc="gF")
                            p.do("vector", (lambda m_in=m_in: (lambda e: e.tensor_mul(
                                g2_t[:], g_t[:], m_in[:])))(), waits=["gF"], inc="gF")
                            p.do("vector", (lambda kt=kt, cct=cct, sel=sel: (lambda e: e.tensor_scalar_add(
                                t1_t[:], cct[:], pconsts[:, kt * 2 + sel: kt * 2 + sel + 1])))(),
                                waits=["cF", ("gF", p.cur("gF"))], inc="gF")
                            p.do("vector", (lambda m_out=m_out: (lambda e: e.tensor_mul(
                                t1_t[:], t1_t[:], m_out[:])))(), waits=["gF"], inc="gF")
                            p.do("vector", lambda e: e.tensor_add(g2_t[:], g2_t[:], t1_t[:]),
                                 waits=["gF"], inc="gF")
                            p.do("vector", (lambda kt=kt: (lambda e: e.tensor_add(
                                AR5[:, kt * 128:(kt + 1) * 128], ps_s[:, 0:128], g2_t[:])))(),
                                waits=["mms", "gF"], inc="sevt")
                        p.do("vector", (lambda kt=kt: (lambda e: e.reduce_max(
                            maxcols[:, kt:kt + 1], AR5[:, kt * 128:(kt + 1) * 128], axis=AX.X)))(),
                            waits=["sevt"], inc="sevt")
                    p.do("vector", lambda e: e.reduce_max(colmax[:], maxcols[:], axis=AX.X),
                         waits=["sevt"], inc="cF")
                    p.do("tensor", lambda e: nc.tensor.transpose(
                        ps_r[0:1, 0:128], colmax[:], identf[:]), waits=["cF"], inc="mma")
                    p.do("vector", lambda e: e.reduce_max(m11[:], ps_r[0:1, 0:128], axis=AX.X),
                         waits=["mma"], inc="cF")
                    p.do("tensor", lambda e: nc.tensor.matmul(
                        ps_bc[:, 0:1], negones_f[:], m11[:], start=True, stop=True),
                        waits=["cF"], inc="mma")
                    p.do("vector", lambda e: e.tensor_copy(negm_sb[:], ps_bc[:, 0:1]),
                         waits=["mma"], inc="cF")
                    for kt in range(16):
                        p.do("scalar", (lambda kt=kt: (lambda e: e.activation(
                            ebf[:, kt * 128:(kt + 1) * 128], AR5[:, kt * 128:(kt + 1) * 128],
                            AF.Exp, bias=negm_sb[:])))(), waits=["cF"], inc="eexp")
                    for kt in range(16):
                        p.do("tensor", (lambda kt=kt: (lambda e: nc.tensor.matmul(
                            ps_r[0:1, 0:128], onesc_bf[:], ebf[:, kt * 128:(kt + 1) * 128],
                            start=(kt == 0), stop=(kt == 15))))(),
                            waits=(["eexp"] if kt == 0 else []), inc=("mmr" if kt == 15 else None))
                    for kt in range(16):
                        p.do("tensor", (lambda kt=kt, h=h: (lambda e: nc.tensor.matmul(
                            ps_c[0:64, 0:128], AR4[:, kt * 1024 + h * 64: kt * 1024 + h * 64 + 64],
                            ebf[:, kt * 128:(kt + 1) * 128],
                            start=(kt == 0), stop=(kt == 15))))(),
                            waits=(["eexp", ("cevt", p.cur("cevt"))] if kt == 0 else []),
                            inc=("mmc" if kt == 15 else None))
                    p.do("vector", lambda e: e.reciprocal(recr[:], ps_r[0:1, 0:128]),
                         waits=["mmr"], inc="cF")
                    p.do("tensor", lambda e: nc.tensor.matmul(
                        ps_bc[:, 128:256], ones_f[:], recr[:], start=True, stop=True),
                        waits=["cF"], inc="mmn")
                    p.do("vector", lambda e: e.tensor_copy(ctxs[:], ps_c[0:64, 0:128]),
                         waits=["mmc"], inc="cevt")
                    p.do("vector", (lambda hb=hb, hc=hc, q0=q0: (lambda e: e.tensor_mul(
                        AR6[hb:hb + 64, hc * 2048 + q0: hc * 2048 + q0 + 128],
                        ctxs[:], ps_bc[0:64, 128:256])))(), waits=["mmn", "cevt"], inc="cF")

            for tt in range(16):
                p.dma("sync", xtile[:], x_d[brow + tt * 128: brow + (tt + 1) * 128, :],
                      waits=[("c", p.cur("c")), ("cF", p.cur("cF")), ("stF", p.cur("stF"))], inc="ld")
                for mc in range(8):
                    p.do("tensor", (lambda mc=mc, tt=tt: (lambda e: nc.tensor.transpose(
                        ps_t[:], AR6[:, mc * 2048 + tt * 128: mc * 2048 + tt * 128 + 128],
                        ident[:])))(), waits=[("cF", p.cur("cF")), ("oevt", p.cur("oevt")), "ld"], inc="mmo")
                    p.do("vector", (lambda mc=mc: (lambda e: e.tensor_add(
                        xtile[:, mc * 128:(mc + 1) * 128], ps_t[:],
                        xtile[:, mc * 128:(mc + 1) * 128])))(), waits=["mmo"], inc="oevt")
                p.dma("gpsimd", y_d[brow + tt * 128: brow + (tt + 1) * 128, :], xtile[:],
                      waits=[("oevt", p.cur("oevt"))], inc="stF")
        p.do("gpsimd", lambda e: e.engine_nop(), waits=["stF"], inc=None)


def prep_attn_weights(d):
    import ml_dtypes
    out = {}
    out["rel_dr"] = np.ascontiguousarray(d["L0_attn_rel_embeddings"])
    out["wq_dr"] = np.ascontiguousarray(d["L0_attn_Wq"])
    out["wk_dr"] = np.ascontiguousarray(d["L0_attn_Wk"])
    out["wv_dr"] = np.ascontiguousarray(d["L0_attn_Wv"])
    out["wpk_dr"] = np.ascontiguousarray(d["L0_attn_Wpk"])
    out["wpq_dr"] = np.ascontiguousarray(d["L0_attn_Wpq"])
    out["bqkv_dr"] = np.ascontiguousarray(np.concatenate(
        [d["L0_attn_bq"].reshape(8, 128).T, d["L0_attn_bk"].reshape(8, 128).T,
         d["L0_attn_bv"].reshape(8, 128).T], axis=1))
    out["bvr_dr"] = np.ascontiguousarray(d["L0_attn_bv"].reshape(1, 1024))
    out["bpq_dr"] = np.ascontiguousarray(d["L0_attn_bpq"].reshape(8, 128).T)
    out["idmat"] = np.eye(128).astype(ml_dtypes.bfloat16)
    out["idmatf"] = np.eye(128).astype(np.float32)
    ii = np.arange(128)[:, None]; jj = np.arange(128)[None, :]
    out["mge_dr"] = (jj >= ii).astype(np.float32)
    out["mlt_dr"] = (jj < ii).astype(np.float32)
    return out



def build_fused(dbg=False):
    nc = bass.Bass("TRN2", target_bir_lowering=False)
    DD = {}

    def ein(name, shp, dt=F32):
        DD[name] = nc.dram_tensor(name, shp, dt, kind="ExternalInput").ap()

    x_d = nc.dram_tensor("x", [T_ROWS, D], F32, kind="ExternalInput").ap()
    y_d = nc.dram_tensor("y", [T_ROWS, D], F32, kind="ExternalOutput").ap()
    xAB = nc.dram_tensor("xAB", [T_ROWS, D], F32, kind="Internal").ap()
    xBC = nc.dram_tensor("xBC", [T_ROWS, D], F32, kind="Internal").ap()
    xCD = nc.dram_tensor("xCD", [T_ROWS, D], F32, kind="Internal").ap()

    ein("idmat", [128, 128], BF16)
    ein("idmatf", [128, 128], F32)
    ein("rel_dr", [1024, 1024]); ein("wq_dr", [1024, 1024]); ein("wk_dr", [1024, 1024])
    ein("wv_dr", [1024, 1024]); ein("wpk_dr", [1024, 1024]); ein("wpq_dr", [1024, 1024])
    ein("bqkv_dr", [128, 24]); ein("bvr_dr", [1, 1024]); ein("bpq_dr", [128, 8])
    ein("mge_dr", [128, 128]); ein("mlt_dr", [128, 128])
    ein("qkvw_dr", [1024, 3072]); ein("qkvbc_dr", [128, 24]); ein("vbr_dr", [1, 1024])
    ein("outw_dr", [1024, 1024]); ein("outbr_dr", [1, 1024])
    ein("n1_dr", [128, 16]); ein("n2_dr", [128, 16])
    ein("w1all_dr", [9 * 1024, 2048]); ein("w2all_dr", [9 * 2048, 1024])
    ein("b1c_dr", [128, 9 * 16]); ein("b2c_dr", [128, 9 * 8])
    ein("centT_dr", [1024, 8]); ein("rb_dr", [1, 8])
    ein("sel8_dr", [8, 8 * 128], BF16); ein("cmaskf_dr", [128, 128])
    ein("ipT_dr", [1024, 4096]); ein("opT_dr", [2048, 1024]); ein("xpT_dr", [2048, 96])
    ein("dpT_dr", [64, 2048]); ein("convw_dr", [128, 64]); ein("convb_dr", [128, 16])
    ein("dtb_dr", [128, 16]); ein("alog_dr", [128, 256]); ein("dvec_dr", [128, 16])
    ein("nw_dr", [128, 8]); ein("selmat", [32, 4096], BF16)
    ein("wD_dr", [D, D]); ein("bD_dr", [1, D])

    p = Plan()
    ctx = ExitStack()
    with ctx:
        emit_blockA(nc, p, DD, x_d, xAB)
        p.barrier()
        emit_blockB(nc, p, DD, xAB, xBC, dbg=dbg)
        p.barrier()
        emit_blockC(nc, p, DD, xBC, xCD)
        p.barrier()
        emit_blockD(nc, p, DD, xCD, y_d)
        p.emit(nc, ctx)
    return nc


def prep_all_weights(d):
    import ml_dtypes
    w = {}
    w.update(prep_attn_weights(d))
    w.update(prep_ffn_weights(d))
    w.update(prep_mamba_weights(d))
    w["wD_dr"] = np.ascontiguousarray(d["L1_ffn_weight"])
    w["bD_dr"] = np.ascontiguousarray(d["L1_ffn_bias"]).reshape(1, 1024)
    w["idmatf"] = np.eye(128).astype(np.float32)
    w["idmat"] = np.eye(128).astype(ml_dtypes.bfloat16)
    return w



_PROGS = {}

def _get_progs():
    if "F" not in _PROGS:
        _PROGS["F"] = build_fused()
    return _PROGS


def kernel(**inputs):
    d = inputs
    prog = _get_progs()["F"]
    x = np.ascontiguousarray(d["L0_attn_hidden_states"].reshape(16, 2048, 1024))
    shards = [np.ascontiguousarray(x[2 * i:2 * i + 2].reshape(T_ROWS, D)) for i in range(NCORES)]
    w = prep_all_weights(d)
    res = run_bass_kernel_spmd(prog, [dict(x=s, **w) for s in shards],
                               core_ids=list(range(NCORES)))
    shards = [np.asarray(r["y"]) for r in res.results]
    out = np.concatenate([s.reshape(2, 2048, 1024) for s in shards], axis=0)
    return out.astype(np.float32)
